# version 05
# baseline (speedup 1.0000x reference)
import numpy as np
import ml_dtypes
import concourse.bass as bass
import concourse.mybir as mybir
from concourse.bass_utils import run_bass_kernel_spmd

F32 = mybir.dt.float32
BF16 = mybir.dt.bfloat16
AF = mybir.ActivationFunctionType
ALU = mybir.AluOpType
AX = mybir.AxisListType

C = 4096
T = 4096
QN = 1024
KVH = 8
INT = 8192
LAT = 2048
NE = 8
EPS = 1e-5
ISQ = float(128) ** -0.5
NCC = C // 128
NTT = T // 512
NQC = QN // 128
NLC = LAT // 128
NIC = INT // 128

SB0 = 18432
KB = 1024


class K:
    def __init__(self):
        self.q = 0
        self.g = 0
        self.t = 0
        self.v = 0
        self.s = 0


class Em:
    def __init__(self, which, eng, ctx):
        self.w = which
        self.e = eng
        self.c = ctx
        self.k = K()
        self.mark = {}

    def wait(self, sem_name, val, only=None):
        if val is None or val <= 0:
            return
        if only is not None and self.w != only:
            return
        self.e.wait_ge(self.c.sems[sem_name], val)

    def dma(self, queue, out, in_, **kw):
        k = self.k
        if queue == 'q':
            k.q += 1
            if self.w == 'q':
                self.c.nc.sync.dma_start(out=out, in_=in_, **kw).then_inc(
                    self.c.sems['q'], 16)
            return 16 * k.q
        else:
            k.g += 1
            if self.w == 'g':
                self.c.nc.gpsimd.dma_start(out=out, in_=in_, **kw).then_inc(
                    self.c.sems['g'], 16)
            return 16 * k.g

    def t_inc(self, inst):
        self.k.t += 1
        if self.w == 't' and inst is not None:
            inst.then_inc(self.c.sems['t'], 1)
        return self.k.t

    def v_inc(self, inst):
        self.k.v += 1
        if self.w == 'v' and inst is not None:
            inst.then_inc(self.c.sems['v'], 1)
        return self.k.v

    def s_inc(self, inst):
        self.k.s += 1
        if self.w == 's' and inst is not None:
            inst.then_inc(self.c.sems['s'], 1)
        return self.k.s

    @property
    def t(self):
        return self.e if self.w == 't' else None

    @property
    def v(self):
        return self.e if self.w == 'v' else None

    @property
    def s(self):
        return self.e if self.w == 's' else None


def build(upto=None, dump=False):
    nc = bass.Bass("TRN2", target_bir_lowering=False)
    skind = "ExternalOutput" if dump else "Internal"

    xb = nc.dram_tensor("xb", [T, C], F32, kind="ExternalInput").ap()
    xq = nc.dram_tensor("xq", [QN, C], F32, kind="ExternalInput").ap()
    wq = nc.dram_tensor("wq", [C, C], F32, kind="ExternalInput").ap()
    bq = nc.dram_tensor("bq", [C], F32, kind="ExternalInput").ap()
    wk = nc.dram_tensor("wk", [C, 1024], F32, kind="ExternalInput").ap()
    bk = nc.dram_tensor("bk", [1024], F32, kind="ExternalInput").ap()
    wv = nc.dram_tensor("wv", [C, 1024], F32, kind="ExternalInput").ap()
    bv = nc.dram_tensor("bv", [1024], F32, kind="ExternalInput").ap()
    wo = nc.dram_tensor("wo", [C, C], F32, kind="ExternalInput").ap()
    bo = nc.dram_tensor("bo", [C], F32, kind="ExternalInput").ap()
    gate_w = nc.dram_tensor("gate_w", [C, NE], F32, kind="ExternalInput").ap()
    up_w = nc.dram_tensor("up_w", [C, INT], F32, kind="ExternalInput").ap()
    down_w = nc.dram_tensor("down_w", [INT, C], F32, kind="ExternalInput").ap()
    fc1_w = nc.dram_tensor("fc1_w", [C, LAT], F32, kind="ExternalInput").ap()
    fc2_w = nc.dram_tensor("fc2_w", [LAT, C], F32, kind="ExternalInput").ap()
    exp_w = nc.dram_tensor("exp_w", [LAT, LAT], F32, kind="ExternalInput").ap()
    ln_g = nc.dram_tensor("ln_g_rep", [128, C], F32, kind="ExternalInput").ap()
    ln_b = nc.dram_tensor("ln_b_rep", [128, C], F32, kind="ExternalInput").ap()
    ident_h = nc.dram_tensor("ident_bf", [128, 128], BF16, kind="ExternalInput").ap()
    identf_h = nc.dram_tensor("ident_f32", [128, 128], F32, kind="ExternalInput").ap()
    ones_h = nc.dram_tensor("ones_bf", [1, 512], BF16, kind="ExternalInput").ap()
    out_h = nc.dram_tensor("out_shard", [QN, C], F32, kind="ExternalOutput").ap()

    s_wk = nc.dram_tensor("s_wk", [NCC, 128, 1024], BF16, kind="Internal").ap()
    s_wv = nc.dram_tensor("s_wv", [NCC, 128, 1024], BF16, kind="Internal").ap()
    s_qt = nc.dram_tensor("s_qt", [NCC, 128, QN], BF16, kind=skind).ap()
    s_r = nc.dram_tensor("s_r", [QN, C], F32, kind=skind).ap()
    s_x1 = nc.dram_tensor("s_x1", [QN, C], F32, kind=skind).ap()
    s_hT = nc.dram_tensor("s_hT", [NIC, 128, QN], BF16, kind=skind).ap()
    s_sh = nc.dram_tensor("s_sh", [NQC, NCC, 128, 128], F32, kind=skind).ap()
    s_rt = nc.dram_tensor("s_rt", [NQC, NCC, 128, 128], F32, kind=skind).ap()

    def sb(name, shape, dtype, off):
        return nc.alloc_sbuf_tensor_at(name, shape, dtype, offset=SB0 + off).ap()

    ident_bf = sb("c_ident", [128, 128], BF16, 188 * KB)
    ident_f = sb("c_identf", [128, 128], F32, 188 * KB + 256)
    ones_bf = sb("c_ones", [1, 512], BF16, 188 * KB + 768)
    eps_col = sb("c_eps", [128, 1], F32, 188 * KB + 768 + 1024)
    stat = sb("c_stat", [128, 32], F32, 190 * KB)
    sstat = sb("c_sstat", [128, 8], F32, 190 * KB + 736)
    max8 = sb("c_max8", [128, 8], F32, 190 * KB + 128)
    sccol = sb("c_sccol", [128, 8], F32, 190 * KB + 160)
    bk_row = sb("c_bk", [1, 1024], BF16, 191 * KB)
    bv_row = sb("c_bv", [1, 1024], BF16, 193 * KB)
    bq_row = sb("c_bq", [1, C], BF16, 195 * KB)
    bo_row = sb("c_bo", [1, C], BF16, 195 * KB)

    xT_own = sb("xT_own", [128, NCC, QN], BF16, 0)
    wq_blk = [sb(f"wq_blk{i}", [128, NCC, 512], BF16, 64 * KB + i * 32 * KB)
              for i in range(2)]
    qt_ev = sb("qt_ev", [128, 1024], BF16, 128 * KB)
    xbf = [sb(f"xbf{i}", [128, C], BF16, 146 * KB + i * 8 * KB) for i in range(2)]

    KT = sb("KT", [128, KVH, T], BF16, 0)
    Vt = sb("Vt", [128, KVH, 32, 129], BF16, 64 * KB)
    xT_t = sb("xT_t", [128, NCC, 256], BF16, 130 * KB)
    wkv_r = [sb(f"wkv_r{i}", [128, 4, 1024], BF16, 162 * KB + i * 8 * KB)
             for i in range(3)]
    pcst = [sb(f"pcst{i}", [128, 1024], BF16, 130 * KB + i * 2 * KB)
            for i in range(2)]

    att = [sb(f"att{h}", [128, 4, QN], BF16,
              (130 * KB if h == 0 else (h - 1) * 8 * KB)) for h in range(KVH)]
    qt_b = [sb(f"qt_b{i}", [128, 4, QN], BF16, 138 * KB + i * 8 * KB)
            for i in range(2)]
    et = [sb(f"et{i}", [128, 1024], BF16, 154 * KB + i * 2 * KB) for i in range(3)]
    aq = [sb(f"aq{i}", [128, 128], BF16, 160 * KB + i * 256) for i in range(2)]
    rdcol = sb("rdcol", [128, 1], F32, 161 * KB)

    wo_blk = [sb(f"wo_blk{i}", [128, NCC, 512], BF16, 64 * KB + i * 32 * KB)
              for i in range(2)]
    xres = [sb(f"xres{i}", [128, 512], F32, 138 * KB + i * 2 * KB) for i in range(8)]
    rev = [sb(f"rev{i}", [128, 512], F32, 154 * KB + i * 2 * KB) for i in range(2)]

    x1T = sb("x1T", [128, NCC, QN], BF16, 0)
    rtile = [sb(f"rtile{i}", [128, C], F32, 64 * KB + i * 16 * KB) for i in range(2)]
    x1bf = sb("x1bf", [128, C], BF16, 96 * KB)
    trash = sb("trash", [128, C], BF16, 104 * KB)
    x1f = sb("x1f", [128, C], F32, 112 * KB)

    xlT = sb("xlT", [128, NLC, QN], BF16, 64 * KB)
    moeT = sb("moeT", [128, NLC, QN], BF16, 96 * KB)
    gate_b = sb("gate_b", [128, NCC, NE], BF16, 160 * KB)
    fc1_blk = [sb(f"fc1_blk{i}", [128, NCC, 256], BF16, 128 * KB + i * 16 * KB)
               for i in range(2)]
    exp_blk = [sb(f"exp_blk{i}", [128, NLC, 256], BF16, 128 * KB + i * 8 * KB)
               for i in range(2)]

    up_blk = [sb(f"up_blk{i}", [128, NCC, 256], BF16, 128 * KB + i * 16 * KB)
              for i in range(2)]
    htmp = [sb(f"htmp{i}", [128, 512], F32, 64 * KB + i * 2 * KB) for i in range(4)]
    hev = [sb(f"hev{i}", [128, 512], BF16, 72 * KB + i * 1 * KB) for i in range(4)]

    hT_a = sb("hT_a", [128, 48, QN], BF16, 0)
    hT_b = sb("hT_b", [128, 16, QN], BF16, 128 * KB)
    dn_blk = [sb(f"dn_blk{i}", [128, 16, 128], BF16, 160 * KB + i * 4 * KB)
              for i in range(2)]
    fc2_blk = [sb(f"fc2_blk{i}", [128, NLC, 128], BF16, 168 * KB + i * 4 * KB)
               for i in range(2)]
    srev = [sb(f"srev{i}", [128, 512], F32, 176 * KB + i * 2 * KB)
            for i in range(4)]

    gam_t = sb("gam_t", [128, C], F32, 0)
    bet_t = sb("bet_t", [128, C], F32, 16 * KB)
    srcm_sh = [sb("srcm_sh0", [128, NCC, 128], F32, 32 * KB)] * 2
    srcm_rt = [sb("srcm_rt0", [128, NCC, 128], F32, 48 * KB)] * 2
    sh_nat2 = [sb(f"sh_nat{i}", [128, C], F32, 64 * KB + i * 32 * KB)
               for i in range(2)]
    rt_nat2 = [sb(f"rt_nat{i}", [128, C], F32, 80 * KB + i * 32 * KB)
               for i in range(2)]
    x1d = [sb(f"x1d{i}", [128, C], F32, 128 * KB + i * 16 * KB) for i in range(2)]
    outt = sb("outt", [128, C], F32, 160 * KB)
    d4trash = sb("d4trash", [128, C], BF16, 176 * KB)

    ps = nc.alloc_psum_tensor("ps", [128, 4096], F32).ap()

    class Ctx:
        pass

    ctx = Ctx()
    ctx.nc = nc
    ctx.sems = {}
    import contextlib
    estack = contextlib.ExitStack()
    for name in ('q', 'g', 't', 'v', 's'):
        ctx.sems[name] = estack.enter_context(nc.semaphore(f"sem_{name}"))

    def bank(i, n=512):
        return ps[:, i * 512: i * 512 + n]

    def vsw(E):
        E.wait('v', E.k.v, only='v')

    def fence(E):
        k = E.k
        E.wait('q', 16 * k.q)
        E.wait('g', 16 * k.g)
        E.wait('t', k.t)
        E.wait('v', k.v)
        E.wait('s', k.s)

    def ph_p0(E):
        E.dma('q', ident_bf[:], ident_h)
        E.dma('q', ident_f[:], identf_h)
        E.dma('q', ones_bf[:], ones_h)
        E.dma('g', bq_row[:], bq[None, :])
        E.dma('g', bk_row[:], bk[None, :])
        E.dma('g', bv_row[:], bv[None, :])
        if E.w == 'v':
            E.e.memset(eps_col[:], EPS)
            E.e.memset(stat[:, 0:2], 1.0)
            E.e.reciprocal(stat[:, 1:2], stat[:, 0:1])
            E.e.max(out=max8[:], in_=stat[:, 0:8])
        st_done = {}
        for wi, (w_src, w_dst) in enumerate(((wk, s_wk), (wv, s_wv))):
            for cc in range(NCC):
                idx = wi * NCC + cc
                b = idx % 2
                E.wait('q', st_done.get(idx - 2), only='g')
                ld = E.dma('g', pcst[b][:], w_src[cc * 128:(cc + 1) * 128, :])
                E.wait('g', ld, only='q')
                st_done[idx] = E.dma('q', w_dst[cc], pcst[b][:])

    def ph_a0(E):
        tr_done = {}
        ev_prev = {}
        for tc in range(NQC):
            b = tc % 2
            E.wait('t', tr_done.get(tc - 2), only='g')
            ld = E.dma('g', xbf[b][:], xq[tc * 128:(tc + 1) * 128, :])
            E.wait('g', ld, only='t')
            for grp in range(8):
                pb = grp % 2
                key = (tc * 8 + grp) - 2
                E.wait('v', ev_prev.get(key), only='t')
                inst = None
                for s4 in range(4):
                    cc = grp * 4 + s4
                    if E.w == 't':
                        inst = E.e.matmul(bank(pb)[:, s4 * 128:(s4 + 1) * 128],
                                          xbf[b][:, cc * 128:(cc + 1) * 128],
                                          ident_bf[:], start=True, stop=True)
                tmm = E.t_inc(inst)
                E.wait('t', tmm, only='v')
                ev = E.v_inc(E.v.tensor_copy(
                    xT_own[:, grp * 4:(grp + 1) * 4, tc * 128:(tc + 1) * 128],
                    bank(pb).rearrange("p (a b) -> p a b", a=4))
                    if E.w == 'v' else None)
                ev_prev[tc * 8 + grp] = ev
            tr_done[tc] = E.k.t

    def ph_a3(E):
        mm_done = {}
        ev_done = {}
        st_done = {}
        pc_st = {}
        pcs = [(w_src, w_dst, cc)
               for w_src, w_dst in ((wk, s_wk), (wv, s_wv))
               for cc in range(NCC)]
        for fcg in range(8):
            b = fcg % 2
            E.wait('t', mm_done.get(fcg - 2), only='g')
            ld = E.dma('g', wq_blk[b][:],
                       wq[:, fcg * 512:(fcg + 1) * 512].rearrange(
                           "(a p) f -> p a f", p=128))

            if E.w == 't':
                E.wait('g', ld)
                E.wait('v', ev_done.get(fcg - 1))
            inst = None
            for cc in range(NCC):
                for fcs in range(4):
                    for qh in range(2):
                        if E.w == 't':
                            inst = E.e.matmul(
                                bank(fcs * 2 + qh),
                                wq_blk[b][:, cc, fcs * 128:(fcs + 1) * 128],
                                xT_own[:, cc, qh * 512:(qh + 1) * 512],
                                start=(cc == 0), stop=False)
            for fcs in range(4):
                fc = fcg * 4 + fcs
                for qh in range(2):
                    if E.w == 't':
                        inst = E.e.matmul(
                            bank(fcs * 2 + qh),
                            bq_row[0:1, fc * 128:(fc + 1) * 128],
                            ones_bf[0:1, :], start=False, stop=True)
            tmm = E.t_inc(inst)
            mm_done[fcg] = tmm
            lastev = None
            for fcs in range(4):
                fc = fcg * 4 + fcs
                if E.w == 'v':
                    E.wait('t', tmm)
                    E.wait('q', st_done.get(fc - 1))
                E.v_inc(E.v.tensor_copy(qt_ev[:, 0:512], bank(fcs * 2))
                        if E.w == 'v' else None)
                ev = E.v_inc(E.v.tensor_copy(qt_ev[:, 512:1024], bank(fcs * 2 + 1))
                             if E.w == 'v' else None)
                E.wait('v', ev, only='q')
                st_done[fc] = E.dma('q', s_qt[fc], qt_ev[:])
                lastev = ev
            ev_done[fcg] = lastev

    def ph_a(E):
        vinst = E.e.memset(Vt[:, :, :, 128:129], 1.0) if E.w == 'v' else None
        E.v_inc(vinst)
        xbf_free = {}
        ring_free = {}
        tr_ev = {}
        ev_k = {}
        kv_free = None
        nld = 0
        for tt in range(16):
            last_tr_ev = None
            for tc in range(2):
                gslot = tt * 2 + tc
                b = gslot % 2
                E.wait('t', xbf_free.get(gslot - 2), only='g')
                ld = E.dma('g', xbf[b][:],
                           xb[tt * 256 + tc * 128: tt * 256 + (tc + 1) * 128, :])
                E.wait('g', ld, only='t')
                for grp in range(8):
                    pb = grp % 2
                    slot = gslot * 8 + grp
                    if E.w == 't':
                        E.wait('v', tr_ev.get(slot - 2))
                        if tt > 0 and tc == 0 and grp < 2:
                            E.wait('v', kv_free)
                    inst = None
                    for s4 in range(4):
                        cc = grp * 4 + s4
                        if E.w == 't':
                            inst = E.e.matmul(bank(pb)[:, s4 * 128:(s4 + 1) * 128],
                                              xbf[b][:, cc * 128:(cc + 1) * 128],
                                              ident_bf[:], start=True, stop=True)
                    tmm = E.t_inc(inst)
                    E.wait('t', tmm, only='v')
                    ev = E.v_inc(E.v.tensor_copy(
                        xT_t[:, grp * 4:(grp + 1) * 4, tc * 128:(tc + 1) * 128],
                        bank(pb).rearrange("p (a b) -> p a b", a=4))
                        if E.w == 'v' else None)
                    tr_ev[slot] = ev
                    last_tr_ev = ev
                xbf_free[gslot] = E.k.t

            for stage in range(2):
                src = (s_wk, s_wv)[stage]
                inst = None
                for blkno in range(8):
                    slot = nld % 3
                    E.wait('t', ring_free.get(nld - 3), only='g')
                    ld = E.dma('g', wkv_r[slot][:],
                               src[blkno * 4:(blkno + 1) * 4].rearrange(
                                   "a p f -> p a f"))
                    if E.w == 't':
                        E.wait('g', ld)
                        if blkno == 0:
                            if stage == 0:
                                E.wait('v', last_tr_ev)
                                E.wait('v', ev_k.get(tt - 1))
                            else:
                                E.wait('v', ev_k[tt])
                    for ccs in range(4):
                        cc = blkno * 4 + ccs
                        if stage == 0:
                            for fc in range(8):
                                if E.w == 't':
                                    inst = E.e.matmul(
                                        bank(fc, 256),
                                        wkv_r[slot][:, ccs, fc * 128:(fc + 1) * 128],
                                        xT_t[:, cc, :],
                                        start=(cc == 0), stop=False)
                        else:
                            for tc in range(2):
                                for nt in range(2):
                                    if E.w == 't':
                                        inst = E.e.matmul(
                                            bank(tc * 2 + nt),
                                            xT_t[:, cc, tc * 128:(tc + 1) * 128],
                                            wkv_r[slot][:, ccs, nt * 512:(nt + 1) * 512],
                                            start=(cc == 0), stop=False)
                    ring_free[nld] = E.t_inc(inst)
                    nld += 1
                if stage == 0:
                    for fc in range(8):
                        if E.w == 't':
                            inst = E.e.matmul(bank(fc, 256),
                                              bk_row[0:1, fc * 128:(fc + 1) * 128],
                                              ones_bf[0:1, 0:256],
                                              start=False, stop=True)
                    tmmk = E.t_inc(inst)
                    evl = None
                    for fc in range(8):
                        E.wait('t', tmmk, only='v')
                        evl = E.v_inc(E.v.tensor_copy(
                            KT[:, fc, tt * 256:(tt + 1) * 256], bank(fc, 256))
                            if E.w == 'v' else None)
                    ev_k[tt] = evl
                else:
                    for tc in range(2):
                        for nt in range(2):
                            if E.w == 't':
                                inst = E.e.matmul(bank(tc * 2 + nt),
                                                  ones_bf[0:1, 0:128],
                                                  bv_row[0:1, nt * 512:(nt + 1) * 512],
                                                  start=False, stop=True)
                    tmmv = E.t_inc(inst)
                    evl = None
                    for tc in range(2):
                        kc = tt * 2 + tc
                        for nt in range(2):
                            E.wait('t', tmmv, only='v')
                            evl = E.v_inc(E.v.tensor_copy(
                                Vt[:, nt * 4:(nt + 1) * 4, kc, 0:128],
                                bank(tc * 2 + nt).rearrange("p (a b) -> p a b", a=4))
                                if E.w == 'v' else None)
                    kv_free = evl

    def ph_b(E):
        def av_off(qs):
            return 3072 + (qs // 2) * 512 + (qs % 2) * 130
        qld = {}
        exp_cnt = {}
        av_cnt = {}
        aq_free = {}
        unit_tr_ev = None
        unit_dv_ev = None
        for h in range(KVH):
            bqt = h % 2
            E.wait('t', qld.get(h - 2), only='q')
            ld = E.dma('q', qt_b[bqt][:],
                       s_qt[4 * h:4 * (h + 1)].rearrange("a p q -> p a q"))
            for gs in range(4):
                for qt in range(2):
                    unit = (h * 4 + gs) * 2 + qt

                    def emit_S(i):
                        pr = (i % 3) * 2
                        if E.w == 't':
                            pe_prev = exp_cnt.get((unit, i - 3)) or \
                                (exp_cnt.get((unit - 1, {0: 15, 1: 13, 2: 14}[i]))
                                 if i < 3 else None)
                            E.wait('s', pe_prev)
                            if i % 3 == 2 and (i == 2 and unit > 0):
                                E.wait('v', unit_tr_ev)
                        inst = None
                        for half in range(2):
                            kc = i * 2 + half
                            if E.w == 't':
                                inst = E.e.matmul(
                                    bank(pr + half),
                                    KT[:, h, kc * 128:(kc + 1) * 128],
                                    qt_b[bqt][:, gs, qt * 512:(qt + 1) * 512],
                                    start=True, stop=True)
                        smm = E.t_inc(inst)
                        if E.w == 's':
                            E.wait('t', smm)
                            avp = av_cnt.get((unit, i - 3)) or \
                                (av_cnt.get((unit - 1, {0: 15, 1: 13, 2: 14}[i]))
                                 if i < 3 else None)
                            E.wait('t', avp)
                        ecnt = E.s_inc(E.s.activation(
                            out=et[i % 3][:], in_=ps[:, pr * 512: pr * 512 + 1024],
                            func=AF.Exp, scale=ISQ) if E.w == 's' else None)
                        exp_cnt[(unit, i)] = ecnt

                    def emit_AV(i):
                        inst = None
                        if E.w == 't':
                            E.wait('s', exp_cnt[(unit, i)])
                            if i == 0 and unit > 0:
                                E.wait('v', unit_dv_ev)
                        for half in range(2):
                            for qs in range(4):
                                if E.w == 't':
                                    inst = E.e.matmul(
                                        ps[:, av_off(qs):av_off(qs) + 129],
                                        et[i % 3][:, half * 512 + qs * 128:
                                                  half * 512 + (qs + 1) * 128],
                                        Vt[:, h, i * 2 + half, :],
                                        start=(i == 0 and half == 0
                                               and qs % 2 == 0),
                                        stop=(i == 15 and half == 1))
                        av_cnt[(unit, i)] = E.t_inc(inst)

                    E.wait('q', ld, only='t')
                    emit_S(0)
                    emit_S(1)
                    for i in range(2, 16):
                        emit_S(i)
                        emit_AV(i - 2)
                    emit_AV(14)
                    emit_AV(15)
                    avl = av_cnt[(unit, 15)]
                    for qs in range(4):
                        baq = qs % 2
                        if E.w == 'v':
                            E.wait('t', avl)
                            prevaq = aq_free.get((unit, qs - 2)) or \
                                (aq_free.get((unit - 1, qs + 2)) if qs < 2 else None)
                            E.wait('t', prevaq)
                        rc = E.v_inc(E.v.reciprocal(
                            rdcol[:],
                            ps[:, av_off(qs) + 128:av_off(qs) + 129])
                            if E.w == 'v' else None)
                        E.wait('v', rc, only='v')
                        dv = E.v_inc(E.v.tensor_scalar(
                            out=aq[baq][:],
                            in0=ps[:, av_off(qs):av_off(qs) + 128],
                            scalar1=rdcol[:], scalar2=None, op0=ALU.mult)
                            if E.w == 'v' else None)
                        if qs == 3:
                            unit_dv_ev = dv
                        if E.w == 't':
                            E.wait('v', dv)
                        inst2 = E.e.matmul(bank(4)[:, qs * 128:(qs + 1) * 128],
                                           aq[baq][:], ident_bf[:],
                                           start=True, stop=True) \
                            if E.w == 't' else None
                        aq_free[(unit, qs)] = E.t_inc(inst2)
                    E.wait('t', aq_free[(unit, 3)], only='v')
                    unit_tr_ev = E.v_inc(E.v.tensor_copy(
                        att[h][:, gs, qt * 512:(qt + 1) * 512], bank(4))
                        if E.w == 'v' else None)
            qld[h] = E.k.t

    def ph_c1(E):
        ldbo = E.dma('g', bo_row[:], bo[None, :])
        mm_done = {}
        ev_done = {}
        xev = {}
        rst = {}
        for fot in range(8):
            b = fot % 2
            E.wait('t', mm_done.get(fot - 2), only='g')
            ld = E.dma('g', wo_blk[b][:],
                       wo[:, fot * 512:(fot + 1) * 512].rearrange(
                           "(a p) f -> p a f", p=128))
            if E.w == 't':
                E.wait('g', ld)
                if fot == 0:
                    E.wait('g', ldbo)
                E.wait('v', ev_done.get(fot - 1))
            inst = None
            for fc in range(NCC):
                h, gs = fc // 4, fc % 4
                for qc in range(8):
                    if E.w == 't':
                        inst = E.e.matmul(bank(qc),
                                          att[h][:, gs, qc * 128:(qc + 1) * 128],
                                          wo_blk[b][:, fc, :],
                                          start=(fc == 0), stop=False)
            for qc in range(8):
                if E.w == 't':
                    inst = E.e.matmul(bank(qc), ones_bf[0:1, 0:128],
                                      bo_row[0:1, fot * 512:(fot + 1) * 512],
                                      start=False, stop=True)
            ld2 = {}
            for qc in range(8):
                E.wait('v', xev.get((fot - 1, qc)), only='q')
                ld2[qc] = E.dma('q', xres[qc][:],
                                xq[qc * 128:(qc + 1) * 128,
                                   fot * 512:(fot + 1) * 512])
            tmm = E.t_inc(inst)
            mm_done[fot] = tmm
            evl = None
            for qc in range(8):
                if E.w == 'v':
                    E.wait('t', tmm)
                    E.wait('q', ld2[qc])
                    E.wait('q', rst.get((fot, qc - 2)) or
                           (rst.get((fot - 1, qc + 6)) if qc < 2 else None))
                ev = E.v_inc(E.v.tensor_add(rev[qc % 2][:], bank(qc), xres[qc][:])
                             if E.w == 'v' else None)
                xev[(fot, qc)] = ev
                E.wait('v', ev, only='q')
                rst[(fot, qc)] = E.dma(
                    'q', s_r[qc * 128:(qc + 1) * 128,
                             fot * 512:(fot + 1) * 512], rev[qc % 2][:])
                evl = ev
            ev_done[fot] = evl

    def ln_stats(E, src, base, sbase, ssq_sink, have_sum=False):
        ssq = E.s_inc(E.s.activation(out=ssq_sink[:], in_=src, func=AF.Square,
                                     accum_out=sstat[:, sbase:sbase + 1])
                      if E.w == 's' else None)
        if not have_sum:
            E.v_inc(E.v.reduce_sum(stat[:, base:base + 1], src, axis=AX.X)
                    if E.w == 'v' else None)
        E.wait('s', ssq, only='v')
        vsw(E)
        E.v_inc(E.v.tensor_scalar_mul(stat[:, base + 1:base + 2],
                                      stat[:, base:base + 1], 1.0 / C)
                if E.w == 'v' else None)
        vsw(E)
        E.v_inc(E.v.tensor_mul(stat[:, base + 2:base + 3],
                               stat[:, base + 1:base + 2],
                               stat[:, base + 1:base + 2])
                if E.w == 'v' else None)
        E.v_inc(E.v.tensor_scalar_mul(stat[:, base + 3:base + 4],
                                      sstat[:, sbase:sbase + 1], 1.0 / C)
                if E.w == 'v' else None)
        vsw(E)
        vv = E.v_inc(E.v.tensor_sub(stat[:, base + 4:base + 5],
                                    stat[:, base + 3:base + 4],
                                    stat[:, base + 2:base + 3])
                     if E.w == 'v' else None)
        E.wait('v', vv, only='s')
        sdv = E.s_inc(E.s.activation(out=sstat[:, sbase + 1:sbase + 2],
                                     in_=stat[:, base + 4:base + 5],
                                     func=AF.Sqrt, bias=eps_col[:])
                      if E.w == 's' else None)
        E.wait('s', sdv, only='v')
        rc = E.v_inc(E.v.reciprocal(stat[:, base + 5:base + 6],
                                    sstat[:, sbase + 1:sbase + 2])
                     if E.w == 'v' else None)
        E.wait('v', rc, only='v')
        return ssq, rc

    def ph_c2(E):
        done = {}
        x1st = {}
        for qc in range(NQC):
            b = qc % 2
            base = 8 * (qc % 2)
            E.wait('v', done.get(qc - 2), only='q')
            ld = E.dma('q', rtile[b][:], s_r[qc * 128:(qc + 1) * 128, :])
            E.wait('q', ld, only='s')
            E.wait('q', ld, only='v')
            ssq, rc = ln_stats(E, rtile[b][:], base, 4 * (qc % 2), trash)
            E.wait('q', x1st.get(qc - 1), only='v')
            x1done = E.v_inc(E.v.tensor_scalar(
                out=x1f[:], in0=rtile[b][:], scalar1=stat[:, base + 1:base + 2],
                scalar2=stat[:, base + 5:base + 6],
                op0=ALU.subtract, op1=ALU.mult) if E.w == 'v' else None)
            E.wait('v', x1done, only='q')
            x1st[qc] = E.dma('q', s_x1[qc * 128:(qc + 1) * 128, :], x1f[:])
            vsw(E)
            cst = E.v_inc(E.v.tensor_copy(x1bf[:], x1f[:]) if E.w == 'v' else None)
            lastev = None
            tr_prev = {}
            for grp in range(8):
                pb = grp % 2
                inst = None
                if E.w == 't':
                    E.wait('v', cst)
                    E.wait('v', getattr(E, f'_c2tr{pb}', None))
                for s4 in range(4):
                    cc = grp * 4 + s4
                    if E.w == 't':
                        inst = E.e.matmul(bank(pb)[:, s4 * 128:(s4 + 1) * 128],
                                          x1bf[:, cc * 128:(cc + 1) * 128],
                                          ident_bf[:], start=True, stop=True)
                tmm = E.t_inc(inst)
                E.wait('t', tmm, only='v')
                ev = E.v_inc(E.v.tensor_copy(
                    x1T[:, grp * 4:(grp + 1) * 4, qc * 128:(qc + 1) * 128],
                    bank(pb).rearrange("p (a b) -> p a b", a=4))
                    if E.w == 'v' else None)
                setattr(E, f'_c2tr{pb}', ev)
                lastev = ev
            done[qc] = lastev

    def ph_d1(E):
        ldg = E.dma('g', gate_b[:], gate_w.rearrange("(a p) e -> p a e", p=128))
        inst = None
        E.wait('g', ldg, only='t')
        for qc in range(8):
            for cc in range(NCC):
                if E.w == 't':
                    inst = E.e.matmul(ps[:, qc * 8:(qc + 1) * 8],
                                      x1T[:, cc, qc * 128:(qc + 1) * 128],
                                      gate_b[:, cc, :],
                                      start=(cc == 0), stop=(cc == NCC - 1))
        tg = E.t_inc(inst)
        for qc in range(8):
            E.wait('t', tg, only='v')
            E.v_inc(E.v.max(out=max8[:], in_=ps[:, qc * 8:(qc + 1) * 8])
                    if E.w == 'v' else None)
            vsw(E)
            E.v_inc(E.v.tensor_add(sccol[:, qc:qc + 1], max8[:, 0:1],
                                   max8[:, 1:2]) if E.w == 'v' else None)
        scv = E.k.v

        mm_done = {}
        ev_done = {}
        for lcg in range(8):
            b = lcg % 2
            E.wait('t', mm_done.get(lcg - 2), only='g')
            ld = E.dma('g', fc1_blk[b][:],
                       fc1_w[:, lcg * 256:(lcg + 1) * 256].rearrange(
                           "(a p) f -> p a f", p=128))
            inst = None
            if E.w == 't':
                E.wait('g', ld)
                E.wait('v', ev_done.get(lcg - 2))
                if lcg == 0:
                    E.wait('v', scv)
            pbase = (lcg % 2) * 4
            for cc in range(NCC):
                for ls in range(2):
                    for qh in range(2):
                        if E.w == 't':
                            inst = E.e.matmul(
                                bank(pbase + ls * 2 + qh),
                                fc1_blk[b][:, cc, ls * 128:(ls + 1) * 128],
                                x1T[:, cc, qh * 512:(qh + 1) * 512],
                                start=(cc == 0), stop=(cc == NCC - 1))
            tmm = E.t_inc(inst)
            mm_done[lcg] = tmm
            evl = None
            for ls in range(2):
                lc = lcg * 2 + ls
                for qh in range(2):
                    E.wait('t', tmm, only='v')
                    evl = E.v_inc(E.v.tensor_copy(
                        xlT[:, lc, qh * 512:(qh + 1) * 512],
                        bank(pbase + ls * 2 + qh)) if E.w == 'v' else None)
            ev_done[lcg] = evl
        fc1_mm_last = mm_done[7]
        fc1_ev_last = ev_done[7]

        mm_done = {}
        ev_done = {}
        for mcg in range(8):
            b = mcg % 2
            if E.w == 'g':
                E.wait('t', mm_done.get(mcg - 2))
                if mcg < 2:
                    E.wait('t', mm_done.get(6) or fc1_mm_last)
            ld = E.dma('g', exp_blk[b][:],
                       exp_w[:, mcg * 256:(mcg + 1) * 256].rearrange(
                           "(a p) f -> p a f", p=128))
            inst = None
            if E.w == 't':
                E.wait('g', ld)
                E.wait('v', ev_done.get(mcg - 2))
                if mcg == 0:
                    E.wait('v', fc1_ev_last)
            pbase = (mcg % 2) * 4
            for lc in range(NLC):
                for ms in range(2):
                    for qh in range(2):
                        if E.w == 't':
                            inst = E.e.matmul(
                                bank(pbase + ms * 2 + qh),
                                exp_blk[b][:, lc, ms * 128:(ms + 1) * 128],
                                xlT[:, lc, qh * 512:(qh + 1) * 512],
                                start=(lc == 0), stop=(lc == NLC - 1))
            tmm = E.t_inc(inst)
            mm_done[mcg] = tmm
            evl = None
            for ms in range(2):
                mc = mcg * 2 + ms
                for qh in range(2):
                    E.wait('t', tmm, only='v')
                    evl = E.v_inc(E.v.tensor_copy(
                        moeT[:, mc, qh * 512:(qh + 1) * 512],
                        bank(pbase + ms * 2 + qh)) if E.w == 'v' else None)
            ev_done[mcg] = evl

    def ph_d2(E):
        mm_done = {}
        ev_done = {}
        hst = {}
        for icg in range(32):
            b = icg % 2
            E.wait('t', mm_done.get(icg - 2), only='g')
            ld = E.dma('g', up_blk[b][:],
                       up_w[:, icg * 256:(icg + 1) * 256].rearrange(
                           "(a p) f -> p a f", p=128))
            inst = None
            if E.w == 't':
                E.wait('g', ld)
                E.wait('v', ev_done.get(icg - 2))
            pbase = (icg % 2) * 4
            for cc in range(NCC):
                for isx in range(2):
                    for qh in range(2):
                        if E.w == 't':
                            inst = E.e.matmul(
                                bank(pbase + isx * 2 + qh),
                                up_blk[b][:, cc, isx * 128:(isx + 1) * 128],
                                x1T[:, cc, qh * 512:(qh + 1) * 512],
                                start=(cc == 0), stop=(cc == NCC - 1))
            tmm = E.t_inc(inst)
            mm_done[icg] = tmm
            evl = None
            for isx in range(2):
                ic = icg * 2 + isx
                for qh in range(2):
                    bb = isx * 2 + qh
                    if E.w == 'v':
                        E.wait('t', tmm)
                        E.wait('q', hst.get((icg - 1, isx, qh)))
                    E.v_inc(E.v.tensor_scalar_max(
                        htmp[bb][:], bank(pbase + isx * 2 + qh), 0.0)
                        if E.w == 'v' else None)
                    vsw(E)
                    evl = E.v_inc(E.v.tensor_mul(hev[bb][:], htmp[bb][:],
                                                 htmp[bb][:])
                                  if E.w == 'v' else None)
                    E.wait('v', evl, only='q')
                    hst[(icg, isx, qh)] = E.dma(
                        'q', s_hT[ic][:, qh * 512:(qh + 1) * 512], hev[bb][:])
            ev_done[icg] = evl

    def ph_d3(E):
        hld = {}
        for ic in range(NIC):
            dst = hT_a[:, ic, :] if ic < 48 else hT_b[:, ic - 48, :]
            hld[ic] = E.dma('q', dst, s_hT[ic])
        sub_mm = {}
        mmf = {}
        ev_done = {}
        sst = {}
        for foc in range(NCC):
            bfc = foc % 2
            E.wait('t', mmf.get(foc - 2), only='g')
            fld = E.dma('g', fc2_blk[bfc][:],
                        fc2_w[:, foc * 128:(foc + 1) * 128].rearrange(
                            "(a p) f -> p a f", p=128))
            inst = None
            pbase = (foc % 2) * 4
            if E.w == 't':
                E.wait('v', ev_done.get(foc - 2))
            subld = None
            for ic in range(NIC):
                sub = ic // 16
                sslot = foc * 4 + sub
                if ic % 16 == 0:
                    E.wait('t', sub_mm.get(sslot - 2), only='g')
                    subld = E.dma(
                        'g', dn_blk[sslot % 2][:],
                        down_w[sub * 2048:(sub + 1) * 2048,
                               foc * 128:(foc + 1) * 128].rearrange(
                            "(a p) f -> p a f", p=128))
                if E.w == 't':
                    if ic % 16 == 0:
                        E.wait('g', subld)
                    if foc == 0:
                        E.wait('q', hld[ic])
                ht = hT_a[:, ic, :] if ic < 48 else hT_b[:, ic - 48, :]
                for qh in range(2):
                    if E.w == 't':
                        inst = E.e.matmul(bank(pbase + qh),
                                          dn_blk[sslot % 2][:, ic % 16, :],
                                          ht[:, qh * 512:(qh + 1) * 512],
                                          start=(ic == 0),
                                          stop=(ic == NIC - 1))
                if ic % 16 == 15:
                    sub_mm[sslot] = E.t_inc(inst)
            for lc in range(NLC):
                if E.w == 't' and lc == 0:
                    E.wait('g', fld)
                for qh in range(2):
                    if E.w == 't':
                        inst = E.e.matmul(bank(pbase + 2 + qh),
                                          fc2_blk[bfc][:, lc, :],
                                          moeT[:, lc, qh * 512:(qh + 1) * 512],
                                          start=(lc == 0), stop=(lc == NLC - 1))
            tmm = E.t_inc(inst)
            mmf[foc] = tmm
            evl = None
            for kind in range(2):
                dst = (s_sh, s_rt)[kind]
                for qh in range(2):
                    bb = kind * 2 + qh
                    if E.w == 'v':
                        E.wait('t', tmm)
                        E.wait('q', sst.get((foc - 1, kind, qh)))
                    evl = E.v_inc(E.v.tensor_copy(
                        srev[bb][:], bank(pbase + kind * 2 + qh))
                        if E.w == 'v' else None)
                    E.wait('v', evl, only='q')
                    st = None
                    for j in range(4):
                        st = E.dma('q', dst[qh * 4 + j][foc],
                                   srev[bb][:, j * 128:(j + 1) * 128])
                    sst[(foc, kind, qh)] = st
            ev_done[foc] = evl

    def ph_d4(E):
        gld = E.dma('q', gam_t[:], ln_g)
        bld = E.dma('q', bet_t[:], ln_b)
        done = {}
        ost = {}
        trdone_at = {}
        ssq_at = {}
        for qc in range(NQC):
            b = qc % 2
            sh_nat = sh_nat2[b]
            rt_nat = rt_nat2[b]
            E.wait('t', trdone_at.get(qc - 1), only='q')
            ldh = E.dma('q', srcm_sh[0][:],
                        s_sh[qc].rearrange("a p q -> p a q"))
            ldr = E.dma('q', srcm_rt[0][:],
                        s_rt[qc].rearrange("a p q -> p a q"))
            E.wait('v', done.get(qc - 2), only='q')
            ldx = E.dma('q', x1d[b][:], s_x1[qc * 128:(qc + 1) * 128, :])
            if E.w == 'v':
                E.wait('q', max(ldx, bld))
            bx1 = E.v_inc(E.v.tensor_add(x1d[b][:], x1d[b][:], bet_t[:])
                          if E.w == 'v' else None)
            for grp in range(16):
                pb = grp % 2
                srcm = srcm_sh[0] if grp < 8 else srcm_rt[0]
                nat = sh_nat if grp < 8 else rt_nat
                g8 = grp % 8
                inst = None
                if E.w == 't':
                    E.wait('q', ldh if grp < 8 else ldr)
                    E.wait('v', getattr(E, f'_d4tr{pb}', None))
                for s4 in range(4):
                    foc = g8 * 4 + s4
                    if E.w == 't':
                        inst = E.e.transpose(bank(pb)[:, s4 * 128:(s4 + 1) * 128],
                                             srcm[:, foc, :], ident_f[:])
                tmm = E.t_inc(inst)
                if E.w == 'v':
                    E.wait('t', tmm)
                    if grp == 0:
                        E.wait('s', ssq_at.get(qc - 2))
                ev = E.v_inc(E.v.tensor_copy(
                    nat[:, g8 * 512:(g8 + 1) * 512], bank(pb))
                    if E.w == 'v' else None)
                setattr(E, f'_d4tr{pb}', ev)
            trdone_at[qc] = E.k.t
            trdone = E.k.v
            vsw(E)
            cmb = E.v_inc(E.v.scalar_tensor_tensor(
                out=rt_nat[:], in0=rt_nat[:], scalar=sccol[:, qc:qc + 1],
                in1=sh_nat[:], op0=ALU.mult, op1=ALU.add,
                accum_out=stat[:, 0:1]) if E.w == 'v' else None)
            E.wait('v', cmb, only='s')
            vsw(E)
            ssq1, _ = ln_stats(E, rt_nat[:], 0, 0, d4trash, have_sum=True)
            E.wait('q', gld, only='v')
            E.v_inc(E.v.scalar_tensor_tensor(
                out=sh_nat[:], in0=rt_nat[:], scalar=stat[:, 1:2], in1=gam_t[:],
                op0=ALU.subtract, op1=ALU.mult) if E.w == 'v' else None)
            vsw(E)
            p2 = E.v_inc(E.v.scalar_tensor_tensor(
                out=x1d[b][:], in0=sh_nat[:], scalar=stat[:, 5:6], in1=x1d[b][:],
                op0=ALU.mult, op1=ALU.add, accum_out=stat[:, 16:17])
                if E.w == 'v' else None)
            E.wait('v', p2, only='s')
            vsw(E)
            ssq2, _ = ln_stats(E, x1d[b][:], 16, 4, d4trash, have_sum=True)
            ssq_at[qc] = ssq2
            E.wait('q', ost.get(qc - 1), only='v')
            odone = E.v_inc(E.v.tensor_scalar(
                out=outt[:], in0=x1d[b][:], scalar1=stat[:, 17:18],
                scalar2=stat[:, 21:22], op0=ALU.subtract, op1=ALU.mult)
                if E.w == 'v' else None)
            E.wait('v', odone, only='q')
            ost[qc] = E.dma('q', out_h[qc * 128:(qc + 1) * 128, :], outt[:])
            done[qc] = odone

    dump_specs = {
        'A0': [('xT_own', xT_own)],
        'A': [('KT', KT), ('Vt', Vt)],
        'B': [(f'att{h}', att[h]) for h in range(KVH)],
        'C2': [('x1T', x1T)],
        'D1': [('xlT', xlT), ('moeT', moeT), ('sccol', sccol)],
    }
    dump_drams = {}
    if dump and upto in dump_specs:
        for name, tile in dump_specs[upto]:
            dump_drams[name] = nc.dram_tensor(
                "dbg_" + name, list(tile.shape), tile.dtype,
                kind="ExternalOutput").ap()

    def ph_dump(E):
        for name, tile in dump_specs[upto]:
            E.dma('q', dump_drams[name], tile[:])

    phase_list = [('P0', ph_p0), ('A0', ph_a0), ('A3', ph_a3), ('A', ph_a),
                  ('B', ph_b), ('C1', ph_c1), ('C2', ph_c2), ('D1', ph_d1),
                  ('D2', ph_d2), ('D3', ph_d3), ('D4', ph_d4)]
    if upto is not None:
        names = [p[0] for p in phase_list]
        phase_list = phase_list[:names.index(upto) + 1]
        if dump and upto in dump_specs:
            phase_list.append(('DUMP', ph_dump))

    def run_all(which, eng):
        E = Em(which, eng, ctx)
        for name, fn in phase_list:
            fence(E)
            fn(E)
        if which == 'q':
            eng.wait_ge(ctx.sems['q'], 16 * E.k.q)
        fence(E)

    with estack, nc.Block() as block:
        @block.tensor
        def _(t):
            run_all('t', t)

        @block.vector
        def _(v):
            run_all('v', v)

        @block.scalar
        def _(s):
            run_all('s', s)

        @block.sync
        def _(q):
            run_all('q', q)

        @block.gpsimd
        def _(g):
            run_all('g', g)

    return nc


_IDENT_BF = np.eye(128).astype(ml_dtypes.bfloat16)
_IDENT_F = np.eye(128, dtype=np.float32)
_ONES_BF = np.ones((1, 512), dtype=ml_dtypes.bfloat16)


def make_in_maps(x, shared):
    in_maps = []
    for c in range(8):
        b = c // 4
        q0 = (c % 4) * QN
        m = dict(shared)
        m["xb"] = x[b]
        m["xq"] = np.ascontiguousarray(x[b][q0:q0 + QN])
        in_maps.append(m)
    return in_maps


def make_shared(L0_attn_w_q, L0_attn_b_q, L0_attn_w_k, L0_attn_b_k,
                L0_attn_w_v, L0_attn_b_v, L0_attn_w_o, L0_attn_b_o,
                L0_ffn_gate_w, L0_ffn_up_w, L0_ffn_down_w, L0_ffn_fc1_w,
                L0_ffn_fc2_w, L0_ffn_expert_w, L0_ffn_ln_gamma, L0_ffn_ln_beta):
    f = lambda a: np.ascontiguousarray(np.asarray(a, dtype=np.float32))
    return {
        "wq": f(L0_attn_w_q), "bq": f(L0_attn_b_q),
        "wk": f(L0_attn_w_k), "bk": f(L0_attn_b_k),
        "wv": f(L0_attn_w_v), "bv": f(L0_attn_b_v),
        "wo": f(L0_attn_w_o), "bo": f(L0_attn_b_o),
        "gate_w": f(L0_ffn_gate_w), "up_w": f(L0_ffn_up_w),
        "down_w": f(L0_ffn_down_w), "fc1_w": f(L0_ffn_fc1_w),
        "fc2_w": f(L0_ffn_fc2_w), "exp_w": f(L0_ffn_expert_w),
        "ln_g_rep": np.ascontiguousarray(np.broadcast_to(
            f(L0_ffn_ln_gamma)[None, :], (128, C))),
        "ln_b_rep": np.ascontiguousarray(np.broadcast_to(
            f(L0_ffn_ln_beta)[None, :], (128, C))),
        "ident_bf": _IDENT_BF, "ident_f32": _IDENT_F, "ones_bf": _ONES_BF,
    }


def kernel(L0_attn_x, L0_attn_mask, L0_attn_w_q, L0_attn_b_q, L0_attn_w_k,
           L0_attn_b_k, L0_attn_w_v, L0_attn_b_v, L0_attn_w_o, L0_attn_b_o,
           L0_ffn_gate_w, L0_ffn_up_w, L0_ffn_down_w, L0_ffn_fc1_w,
           L0_ffn_fc2_w, L0_ffn_expert_w, L0_ffn_ln_gamma, L0_ffn_ln_beta):
    x = np.ascontiguousarray(np.asarray(L0_attn_x, dtype=np.float32))
    B = x.shape[0]
    shared = make_shared(L0_attn_w_q, L0_attn_b_q, L0_attn_w_k, L0_attn_b_k,
                         L0_attn_w_v, L0_attn_b_v, L0_attn_w_o, L0_attn_b_o,
                         L0_ffn_gate_w, L0_ffn_up_w, L0_ffn_down_w,
                         L0_ffn_fc1_w, L0_ffn_fc2_w, L0_ffn_expert_w,
                         L0_ffn_ln_gamma, L0_ffn_ln_beta)
    nc = build()
    res = run_bass_kernel_spmd(nc, make_in_maps(x, shared),
                               core_ids=list(range(8)))
    out = np.empty((B, T, C), dtype=np.float32)
    for c in range(8):
        b = c // 4
        q0 = (c % 4) * QN
        out[b, q0:q0 + QN] = res.results[c]["out_shard"]
    return out


# version 10
# speedup vs baseline: 1.2200x; 1.2200x over previous
import numpy as np
import ml_dtypes
import concourse.bass as bass
import concourse.mybir as mybir
from concourse.bass_utils import run_bass_kernel_spmd

F32 = mybir.dt.float32
BF16 = mybir.dt.bfloat16
AF = mybir.ActivationFunctionType
ALU = mybir.AluOpType
AX = mybir.AxisListType

C = 4096
T = 4096
QN = 1024
KVH = 8
INT = 8192
LAT = 2048
NE = 8
EPS = 1e-5
ISQ = float(128) ** -0.5
NCC = C // 128
NTT = T // 512
NQC = QN // 128
NLC = LAT // 128
NIC = INT // 128

SB0 = 18432
KB = 1024


class K:
    def __init__(self):
        self.q = 0
        self.g = 0
        self.t = 0
        self.v = 0
        self.s = 0


class Em:
    def __init__(self, which, eng, ctx):
        self.w = which
        self.e = eng
        self.c = ctx
        self.k = K()
        self.mark = {}

    def wait(self, sem_name, val, only=None):
        if val is None or val <= 0:
            return
        if only is not None and self.w != only:
            return
        self.e.wait_ge(self.c.sems[sem_name], val)

    def dma(self, queue, out, in_, **kw):
        k = self.k
        if queue == 'q':
            k.q += 1
            if self.w == 'q':
                self.c.nc.sync.dma_start(out=out, in_=in_, **kw).then_inc(
                    self.c.sems['q'], 16)
            return 16 * k.q
        else:
            k.g += 1
            if self.w == 'g':
                self.c.nc.gpsimd.dma_start(out=out, in_=in_, **kw).then_inc(
                    self.c.sems['g'], 16)
            return 16 * k.g

    def t_inc(self, inst):
        self.k.t += 1
        if self.w == 't' and inst is not None:
            inst.then_inc(self.c.sems['t'], 1)
        return self.k.t

    def v_inc(self, inst):
        self.k.v += 1
        if self.w == 'v' and inst is not None:
            inst.then_inc(self.c.sems['v'], 1)
        return self.k.v

    def s_inc(self, inst):
        self.k.s += 1
        if self.w == 's' and inst is not None:
            inst.then_inc(self.c.sems['s'], 1)
        return self.k.s

    @property
    def t(self):
        return self.e if self.w == 't' else None

    @property
    def v(self):
        return self.e if self.w == 'v' else None

    @property
    def s(self):
        return self.e if self.w == 's' else None


def build(upto=None, dump=False):
    nc = bass.Bass("TRN2", target_bir_lowering=False)
    skind = "ExternalOutput" if dump else "Internal"

    xb = nc.dram_tensor("xb", [T, C], F32, kind="ExternalInput").ap()
    xq = nc.dram_tensor("xq", [QN, C], F32, kind="ExternalInput").ap()
    wq = nc.dram_tensor("wq", [C, C], F32, kind="ExternalInput").ap()
    bq = nc.dram_tensor("bq", [C], F32, kind="ExternalInput").ap()
    wk = nc.dram_tensor("wk", [C, 1024], F32, kind="ExternalInput").ap()
    bk = nc.dram_tensor("bk", [1024], F32, kind="ExternalInput").ap()
    wv = nc.dram_tensor("wv", [C, 1024], F32, kind="ExternalInput").ap()
    bv = nc.dram_tensor("bv", [1024], F32, kind="ExternalInput").ap()
    wo = nc.dram_tensor("wo", [C, C], F32, kind="ExternalInput").ap()
    bo = nc.dram_tensor("bo", [C], F32, kind="ExternalInput").ap()
    gate_w = nc.dram_tensor("gate_w", [C, NE], F32, kind="ExternalInput").ap()
    up_w = nc.dram_tensor("up_w", [C, INT], F32, kind="ExternalInput").ap()
    down_w = nc.dram_tensor("down_w", [INT, C], F32, kind="ExternalInput").ap()
    fc1_w = nc.dram_tensor("fc1_w", [C, LAT], F32, kind="ExternalInput").ap()
    fc2_w = nc.dram_tensor("fc2_w", [LAT, C], F32, kind="ExternalInput").ap()
    exp_w = nc.dram_tensor("exp_w", [LAT, LAT], F32, kind="ExternalInput").ap()
    ln_g = nc.dram_tensor("ln_g_rep", [128, C], F32, kind="ExternalInput").ap()
    ln_b = nc.dram_tensor("ln_b_rep", [128, C], F32, kind="ExternalInput").ap()
    ident_h = nc.dram_tensor("ident_bf", [128, 128], BF16, kind="ExternalInput").ap()
    identf_h = nc.dram_tensor("ident_f32", [128, 128], F32, kind="ExternalInput").ap()
    ones_h = nc.dram_tensor("ones_bf", [1, 512], BF16, kind="ExternalInput").ap()
    out_h = nc.dram_tensor("out_shard", [QN, C], F32, kind="ExternalOutput").ap()

    s_wk = nc.dram_tensor("s_wk", [NCC, 128, 1024], BF16, kind="Internal").ap()
    s_wv = nc.dram_tensor("s_wv", [NCC, 128, 1024], BF16, kind="Internal").ap()
    s_qt = nc.dram_tensor("s_qt", [NCC, 128, QN], BF16, kind=skind).ap()
    s_r = nc.dram_tensor("s_r", [QN, C], F32, kind=skind).ap()
    s_x1 = nc.dram_tensor("s_x1", [QN, C], F32, kind=skind).ap()
    s_hT = nc.dram_tensor("s_hT", [NIC, 128, QN], BF16, kind=skind).ap()
    s_sh = nc.dram_tensor("s_sh", [NQC, NCC, 128, 128], F32, kind=skind).ap()
    s_rt = nc.dram_tensor("s_rt", [NQC, NCC, 128, 128], F32, kind=skind).ap()

    def sb(name, shape, dtype, off):
        return nc.alloc_sbuf_tensor_at(name, shape, dtype, offset=SB0 + off).ap()

    ident_bf = sb("c_ident", [128, 128], BF16, 188 * KB)
    ident_f = sb("c_identf", [128, 128], F32, 188 * KB + 256)
    ones_bf = sb("c_ones", [1, 512], BF16, 188 * KB + 768)
    eps_col = sb("c_eps", [128, 1], F32, 188 * KB + 768 + 1024)
    stat = sb("c_stat", [128, 32], F32, 190 * KB)
    sstat = sb("c_sstat", [128, 8], F32, 190 * KB + 736)
    max8 = sb("c_max8", [128, 8], F32, 190 * KB + 128)
    sccol = sb("c_sccol", [128, 8], F32, 190 * KB + 160)
    bk_row = sb("c_bk", [1, 1024], BF16, 191 * KB)
    bv_row = sb("c_bv", [1, 1024], BF16, 193 * KB)
    bq_row = sb("c_bq", [1, C], BF16, 195 * KB)
    bo_row = sb("c_bo", [1, C], BF16, 195 * KB)

    xT_own = sb("xT_own", [128, NCC, QN], BF16, 0)
    wq_blk = [sb(f"wq_blk{i}", [128, NCC, 512], BF16, 64 * KB + i * 32 * KB)
              for i in range(2)]
    qt_ev = sb("qt_ev", [128, 1024], BF16, 128 * KB)
    xbf = [sb(f"xbf{i}", [128, C], BF16, 146 * KB + i * 8 * KB) for i in range(2)]

    KT = sb("KT", [128, KVH, T], BF16, 0)
    Vt = sb("Vt", [128, KVH, 32, 129], BF16, 64 * KB)
    xT_t = sb("xT_t", [128, NCC, 256], BF16, 130 * KB)
    wkv_r = [sb(f"wkv_r{i}", [128, 4, 1024], BF16, 162 * KB + i * 8 * KB)
             for i in range(3)]
    pcst = [sb(f"pcst{i}", [128, 1024], BF16, 146 * KB + i * 2 * KB)
            for i in range(2)]

    att = [sb(f"att{h}", [128, 4, QN], BF16,
              (130 * KB if h == 0 else (h - 1) * 8 * KB)) for h in range(KVH)]
    qt_b = [sb(f"qt_b{i}", [128, 4, QN], BF16, 138 * KB + i * 8 * KB)
            for i in range(2)]
    et = [sb(f"et{i}", [128, 1024], BF16, 154 * KB + i * 2 * KB) for i in range(3)]
    aq = [sb(f"aq{i}", [128, 128], BF16, 160 * KB + i * 256) for i in range(2)]
    rdcol = sb("rdcol", [128, 1], F32, 161 * KB)

    wo_blk = [sb(f"wo_blk{i}", [128, NCC, 512], BF16, 64 * KB + i * 32 * KB)
              for i in range(2)]
    xres = [sb(f"xres{i}", [128, 512], F32, 138 * KB + i * 2 * KB) for i in range(8)]
    rev = [sb(f"rev{i}", [128, 512], F32, 154 * KB + i * 2 * KB) for i in range(8)]

    x1T = sb("x1T", [128, NCC, QN], BF16, 0)
    rtile = [sb(f"rtile{i}", [128, C], F32, 64 * KB + i * 16 * KB) for i in range(2)]
    x1bf2 = [sb(f"x1bf{i}", [128, C], BF16, 96 * KB + i * 48 * KB) for i in range(2)]
    trash = sb("trash", [128, C], BF16, 104 * KB)
    x1f2 = [sb(f"x1f{i}", [128, C], F32, 112 * KB + i * 16 * KB) for i in range(2)]

    xlT = sb("xlT", [128, NLC, QN], BF16, 64 * KB)
    moeT = sb("moeT", [128, NLC, QN], BF16, 96 * KB)
    gate_b = sb("gate_b", [128, NCC, NE], BF16, 160 * KB)
    fc1_blk = [sb(f"fc1_blk{i}", [128, NCC, 256], BF16, 128 * KB + i * 16 * KB)
               for i in range(2)]
    exp_blk = [sb(f"exp_blk{i}", [128, NLC, 256], BF16, 128 * KB + i * 8 * KB)
               for i in range(2)]

    up_blk = [sb(f"up_blk{i}", [128, NCC, 256], BF16, 128 * KB + i * 16 * KB)
              for i in range(2)]
    htmp = [sb(f"htmp{i}", [128, 512], F32, 64 * KB + i * 2 * KB) for i in range(4)]
    hev = [sb(f"hev{i}", [128, 512], BF16, 72 * KB + i * 1 * KB) for i in range(4)]

    hT_a = sb("hT_a", [128, 48, QN], BF16, 0)
    hT_b = sb("hT_b", [128, 16, QN], BF16, 128 * KB)
    dn_blk = [sb(f"dn_blk{i}", [128, 16, 128], BF16, 160 * KB + i * 4 * KB)
              for i in range(2)]
    fc2_blk = [sb(f"fc2_blk{i}", [128, NLC, 128], BF16, 168 * KB + i * 4 * KB)
               for i in range(2)]
    srev = [sb(f"srev{i}", [128, 512], F32, 176 * KB + i * 2 * KB)
            for i in range(4)]

    gam_t = sb("gam_t", [128, C], F32, 0)
    bet_t = sb("bet_t", [128, C], F32, 16 * KB)
    srcm_sh = [sb("srcm_sh0", [128, NCC, 128], F32, 32 * KB)] * 2
    srcm_rt = [sb("srcm_rt0", [128, NCC, 128], F32, 48 * KB)] * 2
    sh_nat2 = [sb(f"sh_nat{i}", [128, C], F32, 64 * KB + i * 32 * KB)
               for i in range(2)]
    rt_nat2 = [sb(f"rt_nat{i}", [128, C], F32, 80 * KB + i * 32 * KB)
               for i in range(2)]
    x1d = [sb(f"x1d{i}", [128, C], F32, 128 * KB + i * 16 * KB) for i in range(2)]
    outt = sb("outt", [128, C], F32, 160 * KB)
    d4trash = sb("d4trash", [128, C], BF16, 176 * KB)

    ps = nc.alloc_psum_tensor("ps", [128, 4096], F32).ap()

    class Ctx:
        pass

    ctx = Ctx()
    ctx.nc = nc
    ctx.sems = {}
    import contextlib
    estack = contextlib.ExitStack()
    for name in ('q', 'g', 't', 'v', 's'):
        ctx.sems[name] = estack.enter_context(nc.semaphore(f"sem_{name}"))

    def bank(i, n=512):
        return ps[:, i * 512: i * 512 + n]

    def vsw(E):
        E.wait('v', E.k.v, only='v')

    def fence(E):
        k = E.k
        E.wait('q', 16 * k.q)
        E.wait('g', 16 * k.g)
        E.wait('t', k.t)
        E.wait('v', k.v)
        E.wait('s', k.s)

    def ph_p0(E):
        E.dma('q', ident_bf[:], ident_h)
        E.dma('q', ident_f[:], identf_h)
        E.dma('q', ones_bf[:], ones_h)
        E.dma('g', bq_row[:], bq[None, :])
        E.dma('g', bk_row[:], bk[None, :])
        E.dma('g', bv_row[:], bv[None, :])
        if E.w == 'v':
            E.e.memset(eps_col[:], EPS)
        st_done = {}
        for wi, (w_src, w_dst) in enumerate(((wk, s_wk), (wv, s_wv))):
            for cc in range(NCC):
                idx = wi * NCC + cc
                b = idx % 2
                E.wait('q', st_done.get(idx - 2), only='g')
                ld = E.dma('g', pcst[b][:], w_src[cc * 128:(cc + 1) * 128, :])
                E.wait('g', ld, only='q')
                st_done[idx] = E.dma('q', w_dst[cc], pcst[b][:])

    def ph_a0(E):
        tr_done = {}
        ev_prev = {}
        for tc in range(NQC):
            b = tc % 2
            E.wait('t', tr_done.get(tc - 2), only='g')
            ld = E.dma('g', xbf[b][:], xq[tc * 128:(tc + 1) * 128, :])
            E.wait('g', ld, only='t')
            for grp in range(8):
                pb = grp % 2
                key = (tc * 8 + grp) - 2
                E.wait('v', ev_prev.get(key), only='t')
                inst = None
                for s4 in range(4):
                    cc = grp * 4 + s4
                    if E.w == 't':
                        inst = E.e.matmul(bank(pb)[:, s4 * 128:(s4 + 1) * 128],
                                          xbf[b][:, cc * 128:(cc + 1) * 128],
                                          ident_bf[:], start=True, stop=True)
                tmm = E.t_inc(inst)
                E.wait('t', tmm, only='v')
                ev = E.v_inc(E.v.tensor_copy(
                    xT_own[:, grp * 4:(grp + 1) * 4, tc * 128:(tc + 1) * 128],
                    bank(pb).rearrange("p (a b) -> p a b", a=4))
                    if E.w == 'v' else None)
                ev_prev[tc * 8 + grp] = ev
            tr_done[tc] = E.k.t

    def ph_a3(E):
        mm_done = {}
        ev_done = {}
        st_done = {}
        pc_st = {}
        pcs = [(w_src, w_dst, cc)
               for w_src, w_dst in ((wk, s_wk), (wv, s_wv))
               for cc in range(NCC)]
        for fcg in range(8):
            b = fcg % 2
            E.wait('t', mm_done.get(fcg - 2), only='g')
            ld = E.dma('g', wq_blk[b][:],
                       wq[:, fcg * 512:(fcg + 1) * 512].rearrange(
                           "(a p) f -> p a f", p=128))
            if E.w == 't':
                E.wait('g', ld)
                E.wait('v', ev_done.get(fcg - 1))
            inst = None
            for cc in range(NCC):
                for fcs in range(4):
                    for qh in range(2):
                        if E.w == 't':
                            inst = E.e.matmul(
                                bank(fcs * 2 + qh),
                                wq_blk[b][:, cc, fcs * 128:(fcs + 1) * 128],
                                xT_own[:, cc, qh * 512:(qh + 1) * 512],
                                start=(cc == 0), stop=False)
            for fcs in range(4):
                fc = fcg * 4 + fcs
                for qh in range(2):
                    if E.w == 't':
                        inst = E.e.matmul(
                            bank(fcs * 2 + qh),
                            bq_row[0:1, fc * 128:(fc + 1) * 128],
                            ones_bf[0:1, :], start=False, stop=True)
            tmm = E.t_inc(inst)
            mm_done[fcg] = tmm
            lastev = None
            for fcs in range(4):
                fc = fcg * 4 + fcs
                if E.w == 'v':
                    E.wait('t', tmm)
                    E.wait('q', st_done.get(fc - 1))
                E.v_inc(E.v.tensor_copy(qt_ev[:, 0:512], bank(fcs * 2))
                        if E.w == 'v' else None)
                ev = E.v_inc(E.v.tensor_copy(qt_ev[:, 512:1024], bank(fcs * 2 + 1))
                             if E.w == 'v' else None)
                E.wait('v', ev, only='q')
                st_done[fc] = E.dma('q', s_qt[fc], qt_ev[:])
                lastev = ev
            ev_done[fcg] = lastev

    def ph_a(E):
        vinst = E.e.memset(Vt[:, :, :, 128:129], 1.0) if E.w == 'v' else None
        E.v_inc(vinst)
        xbf_free = {}
        ring_free = {}
        tr_ev = {}
        ev_k = {}
        kv_free = None
        nld = 0
        for tt in range(16):
            last_tr_ev = None
            for tc in range(2):
                gslot = tt * 2 + tc
                b = gslot % 2
                E.wait('t', xbf_free.get(gslot - 2), only='g')
                ld = E.dma('g', xbf[b][:],
                           xb[tt * 256 + tc * 128: tt * 256 + (tc + 1) * 128, :])
                E.wait('g', ld, only='t')
                for grp in range(8):
                    pb = grp % 2
                    slot = gslot * 8 + grp
                    if E.w == 't':
                        E.wait('v', tr_ev.get(slot - 2))
                        if tt > 0 and tc == 0 and grp < 2:
                            E.wait('v', kv_free)
                    inst = None
                    for s4 in range(4):
                        cc = grp * 4 + s4
                        if E.w == 't':
                            inst = E.e.matmul(bank(pb)[:, s4 * 128:(s4 + 1) * 128],
                                              xbf[b][:, cc * 128:(cc + 1) * 128],
                                              ident_bf[:], start=True, stop=True)
                    tmm = E.t_inc(inst)
                    E.wait('t', tmm, only='v')
                    ev = E.v_inc(E.v.tensor_copy(
                        xT_t[:, grp * 4:(grp + 1) * 4, tc * 128:(tc + 1) * 128],
                        bank(pb).rearrange("p (a b) -> p a b", a=4))
                        if E.w == 'v' else None)
                    tr_ev[slot] = ev
                    last_tr_ev = ev
                xbf_free[gslot] = E.k.t

            for stage in range(2):
                src = (s_wk, s_wv)[stage]
                inst = None
                for blkno in range(8):
                    slot = nld % 3
                    E.wait('t', ring_free.get(nld - 3), only='g')
                    ld = E.dma('g', wkv_r[slot][:],
                               src[blkno * 4:(blkno + 1) * 4].rearrange(
                                   "a p f -> p a f"))
                    if E.w == 't':
                        E.wait('g', ld)
                        if blkno == 0:
                            if stage == 0:
                                E.wait('v', last_tr_ev)
                                E.wait('v', ev_k.get(tt - 1))
                            else:
                                E.wait('v', ev_k[tt])
                    for ccs in range(4):
                        cc = blkno * 4 + ccs
                        if stage == 0:
                            for fc in range(8):
                                if E.w == 't':
                                    inst = E.e.matmul(
                                        bank(fc, 256),
                                        wkv_r[slot][:, ccs, fc * 128:(fc + 1) * 128],
                                        xT_t[:, cc, :],
                                        start=(cc == 0), stop=False)
                        else:
                            for tc in range(2):
                                for nt in range(2):
                                    if E.w == 't':
                                        inst = E.e.matmul(
                                            bank(tc * 2 + nt),
                                            xT_t[:, cc, tc * 128:(tc + 1) * 128],
                                            wkv_r[slot][:, ccs, nt * 512:(nt + 1) * 512],
                                            start=(cc == 0), stop=False)
                    ring_free[nld] = E.t_inc(inst)
                    nld += 1
                if stage == 0:
                    for fc in range(8):
                        if E.w == 't':
                            inst = E.e.matmul(bank(fc, 256),
                                              bk_row[0:1, fc * 128:(fc + 1) * 128],
                                              ones_bf[0:1, 0:256],
                                              start=False, stop=True)
                    tmmk = E.t_inc(inst)
                    evl = None
                    for fc in range(8):
                        E.wait('t', tmmk, only='v')
                        evl = E.v_inc(E.v.tensor_copy(
                            KT[:, fc, tt * 256:(tt + 1) * 256], bank(fc, 256))
                            if E.w == 'v' else None)
                    ev_k[tt] = evl
                else:
                    for tc in range(2):
                        for nt in range(2):
                            if E.w == 't':
                                inst = E.e.matmul(bank(tc * 2 + nt),
                                                  ones_bf[0:1, 0:128],
                                                  bv_row[0:1, nt * 512:(nt + 1) * 512],
                                                  start=False, stop=True)
                    tmmv = E.t_inc(inst)
                    evl = None
                    for tc in range(2):
                        kc = tt * 2 + tc
                        for nt in range(2):
                            E.wait('t', tmmv, only='v')
                            evl = E.v_inc(E.v.tensor_copy(
                                Vt[:, nt * 4:(nt + 1) * 4, kc, 0:128],
                                bank(tc * 2 + nt).rearrange("p (a b) -> p a b", a=4))
                                if E.w == 'v' else None)
                    kv_free = evl

    def ph_b(E):
        def av_off(qs):
            return 3072 + (qs // 2) * 512 + (qs % 2) * 130
        qld = {}
        exp_cnt = {}
        av_cnt = {}
        aq_free = {}
        unit_tr_ev = None
        unit_dv_ev = None
        for h in range(KVH):
            bqt = h % 2
            E.wait('t', qld.get(h - 2), only='q')
            ld = E.dma('q', qt_b[bqt][:],
                       s_qt[4 * h:4 * (h + 1)].rearrange("a p q -> p a q"))
            for gs in range(4):
                for qt in range(2):
                    unit = (h * 4 + gs) * 2 + qt

                    def emit_S(i):
                        pr = (i % 3) * 2
                        if E.w == 't':
                            pe_prev = exp_cnt.get((unit, i - 3)) or \
                                (exp_cnt.get((unit - 1, {0: 15, 1: 13, 2: 14}[i]))
                                 if i < 3 else None)
                            E.wait('s', pe_prev)
                        inst = None
                        for half in range(2):
                            kc = i * 2 + half
                            if E.w == 't':
                                inst = E.e.matmul(
                                    bank(pr + half),
                                    KT[:, h, kc * 128:(kc + 1) * 128],
                                    qt_b[bqt][:, gs, qt * 512:(qt + 1) * 512],
                                    start=True, stop=True)
                        smm = E.t_inc(inst)
                        if E.w == 's':
                            E.wait('t', smm)
                            avp = av_cnt.get((unit, i - 3)) or \
                                (av_cnt.get((unit - 1, {0: 15, 1: 13, 2: 14}[i]))
                                 if i < 3 else None)
                            E.wait('t', avp)
                        ecnt = E.s_inc(E.s.activation(
                            out=et[i % 3][:], in_=ps[:, pr * 512: pr * 512 + 1024],
                            func=AF.Exp, scale=ISQ) if E.w == 's' else None)
                        exp_cnt[(unit, i)] = ecnt

                    def emit_AV(i):
                        inst = None
                        if E.w == 't':
                            E.wait('s', exp_cnt[(unit, i)])
                            if i == 0 and unit > 0:
                                E.wait('v', unit_dv_ev)
                        for half in range(2):
                            for qs in range(4):
                                if E.w == 't':
                                    inst = E.e.matmul(
                                        ps[:, av_off(qs):av_off(qs) + 129],
                                        et[i % 3][:, half * 512 + qs * 128:
                                                  half * 512 + (qs + 1) * 128],
                                        Vt[:, h, i * 2 + half, :],
                                        start=(i == 0 and half == 0
                                               and qs % 2 == 0),
                                        stop=(i == 15 and half == 1))
                        av_cnt[(unit, i)] = E.t_inc(inst)

                    E.wait('q', ld, only='t')
                    emit_S(0)
                    emit_S(1)
                    for i in range(2, 16):
                        emit_S(i)
                        emit_AV(i - 2)
                    emit_AV(14)
                    emit_AV(15)
                    avl = av_cnt[(unit, 15)]
                    for qs in range(4):
                        E.wait('t', avl, only='v')
                        rc = E.v_inc(E.v.reciprocal(
                            rdcol[:],
                            ps[:, av_off(qs) + 128:av_off(qs) + 129])
                            if E.w == 'v' else None)
                        E.wait('v', rc, only='v')
                        dv = E.v_inc(E.v.tensor_scalar(
                            out=att[h][:, gs, qt * 512 + qs * 128:
                                       qt * 512 + (qs + 1) * 128],
                            in0=ps[:, av_off(qs):av_off(qs) + 128],
                            scalar1=rdcol[:], scalar2=None, op0=ALU.mult)
                            if E.w == 'v' else None)
                        if qs == 3:
                            unit_dv_ev = dv
            qld[h] = E.k.t

    def ph_c1(E):
        ldbo = E.dma('g', bo_row[:], bo[None, :])
        trp = {}
        for blk in range(64):
            h, gs, qt = blk // 8, (blk // 2) % 4, blk % 2
            pb = blk % 2
            inst = None
            if E.w == 't':
                E.wait('v', trp.get(blk - 2))
            for qs in range(4):
                if E.w == 't':
                    inst = E.e.matmul(
                        bank(pb)[:, qs * 128:(qs + 1) * 128],
                        att[h][:, gs, qt * 512 + qs * 128:
                               qt * 512 + (qs + 1) * 128],
                        ident_bf[:], start=True, stop=True)
            t1 = E.t_inc(inst)
            E.wait('t', t1, only='v')
            trp[blk] = E.v_inc(E.v.tensor_copy(
                att[h][:, gs, qt * 512:(qt + 1) * 512], bank(pb))
                if E.w == 'v' else None)
        arena_tr = E.k.v
        mm_done = {}
        ev_done = {}
        xev = {}
        rst = {}
        for fot in range(8):
            b = fot % 2
            E.wait('t', mm_done.get(fot - 2), only='g')
            ld = E.dma('g', wo_blk[b][:],
                       wo[:, fot * 512:(fot + 1) * 512].rearrange(
                           "(a p) f -> p a f", p=128))
            if E.w == 't':
                E.wait('g', ld)
                if fot == 0:
                    E.wait('g', ldbo)
                    E.wait('v', arena_tr)
                E.wait('v', ev_done.get(fot - 1))
            inst = None
            for fc in range(NCC):
                h, gs = fc // 4, fc % 4
                for qc in range(8):
                    if E.w == 't':
                        inst = E.e.matmul(bank(qc),
                                          att[h][:, gs, qc * 128:(qc + 1) * 128],
                                          wo_blk[b][:, fc, :],
                                          start=(fc == 0), stop=False)
            for qc in range(8):
                if E.w == 't':
                    inst = E.e.matmul(bank(qc), ones_bf[0:1, 0:128],
                                      bo_row[0:1, fot * 512:(fot + 1) * 512],
                                      start=False, stop=True)
            ld2 = {}
            for qc in range(8):
                E.wait('v', xev.get((fot - 1, qc)), only='q')
                ld2[qc] = E.dma('q', xres[qc][:],
                                xq[qc * 128:(qc + 1) * 128,
                                   fot * 512:(fot + 1) * 512])
            tmm = E.t_inc(inst)
            mm_done[fot] = tmm
            evl = None
            for qc in range(8):
                if E.w == 'v':
                    E.wait('t', tmm)
                    E.wait('q', ld2[qc])
                    E.wait('q', rst.get((fot - 1, qc)))
                ev = E.v_inc(E.v.tensor_add(rev[qc][:], bank(qc), xres[qc][:])
                             if E.w == 'v' else None)
                xev[(fot, qc)] = ev
                E.wait('v', ev, only='q')
                rst[(fot, qc)] = E.dma(
                    'q', s_r[qc * 128:(qc + 1) * 128,
                             fot * 512:(fot + 1) * 512], rev[qc][:])
                evl = ev
            ev_done[fot] = evl

    def ln_stats(E, src, base, sbase, ssq_sink, have_sum=False):
        def ssw():
            E.wait('s', E.k.s, only='s')

        ssq = E.s_inc(E.s.activation(out=ssq_sink[:], in_=src, func=AF.Square,
                                     accum_out=sstat[:, sbase:sbase + 1])
                      if E.w == 's' else None)
        if not have_sum:
            sv = E.v_inc(E.v.reduce_sum(stat[:, base:base + 1], src, axis=AX.X)
                         if E.w == 'v' else None)
            E.wait('v', sv, only='s')
        E.s_inc(E.s.mul(sstat[:, sbase + 1:sbase + 2], stat[:, base:base + 1],
                        1.0 / C) if E.w == 's' else None)
        ssw()
        E.s_inc(E.s.activation(out=sstat[:, sbase + 2:sbase + 3],
                               in_=sstat[:, sbase + 1:sbase + 2],
                               func=AF.Square) if E.w == 's' else None)
        ssw()
        E.s_inc(E.s.mul(sstat[:, sbase + 2:sbase + 3],
                        sstat[:, sbase + 2:sbase + 3], -1.0)
                if E.w == 's' else None)
        E.s_inc(E.s.mul(sstat[:, sbase:sbase + 1], sstat[:, sbase:sbase + 1],
                        1.0 / C) if E.w == 's' else None)
        ssw()
        E.s_inc(E.s.add(sstat[:, sbase:sbase + 1], sstat[:, sbase:sbase + 1],
                        sstat[:, sbase + 2:sbase + 3])
                if E.w == 's' else None)
        ssw()
        sdv = E.s_inc(E.s.activation(out=sstat[:, sbase + 3:sbase + 4],
                                     in_=sstat[:, sbase:sbase + 1],
                                     func=AF.Sqrt, bias=eps_col[:])
                      if E.w == 's' else None)
        E.wait('s', sdv, only='v')
        rc = E.v_inc(E.v.reciprocal(stat[:, base + 5:base + 6],
                                    sstat[:, sbase + 3:sbase + 4])
                     if E.w == 'v' else None)
        E.wait('v', rc, only='v')
        return ssq, rc

    def ph_c2(E):
        done = {}
        x1st = {}
        for qc in range(NQC):
            b = qc % 2
            base = 8 * (qc % 2)
            E.wait('v', done.get(qc - 2), only='q')
            ld = E.dma('q', rtile[b][:], s_r[qc * 128:(qc + 1) * 128, :])
            E.wait('q', ld, only='s')
            E.wait('q', ld, only='v')
            x1f = x1f2[b]
            x1bf = x1bf2[b]
            ssq, rc = ln_stats(E, rtile[b][:], base, 4 * (qc % 2), trash)
            E.wait('q', x1st.get(qc - 2), only='v')
            x1done = E.v_inc(E.v.tensor_scalar(
                out=x1f[:], in0=rtile[b][:],
                scalar1=sstat[:, 4 * (qc % 2) + 1:4 * (qc % 2) + 2],
                scalar2=stat[:, base + 5:base + 6],
                op0=ALU.subtract, op1=ALU.mult) if E.w == 'v' else None)
            E.wait('v', x1done, only='q')
            x1st[qc] = E.dma('q', s_x1[qc * 128:(qc + 1) * 128, :], x1f[:])
            vsw(E)
            cst = E.v_inc(E.v.tensor_copy(x1bf[:], x1f[:]) if E.w == 'v' else None)
            lastev = None
            tr_prev = {}
            for grp in range(8):
                pb = grp % 2
                inst = None
                if E.w == 't':
                    E.wait('v', cst)
                    E.wait('v', getattr(E, f'_c2tr{pb}', None))
                for s4 in range(4):
                    cc = grp * 4 + s4
                    if E.w == 't':
                        inst = E.e.matmul(bank(pb)[:, s4 * 128:(s4 + 1) * 128],
                                          x1bf[:, cc * 128:(cc + 1) * 128],
                                          ident_bf[:], start=True, stop=True)
                tmm = E.t_inc(inst)
                E.wait('t', tmm, only='v')
                ev = E.v_inc(E.v.tensor_copy(
                    x1T[:, grp * 4:(grp + 1) * 4, qc * 128:(qc + 1) * 128],
                    bank(pb).rearrange("p (a b) -> p a b", a=4))
                    if E.w == 'v' else None)
                setattr(E, f'_c2tr{pb}', ev)
                lastev = ev
            done[qc] = lastev

    def ph_d1(E):
        ldg = E.dma('g', gate_b[:], gate_w.rearrange("(a p) e -> p a e", p=128))
        inst = None
        E.wait('g', ldg, only='t')
        for qc in range(8):
            for cc in range(NCC):
                if E.w == 't':
                    inst = E.e.matmul(ps[:, qc * 8:(qc + 1) * 8],
                                      x1T[:, cc, qc * 128:(qc + 1) * 128],
                                      gate_b[:, cc, :],
                                      start=(cc == 0), stop=(cc == NCC - 1))
        tg = E.t_inc(inst)
        for qc in range(8):
            E.wait('t', tg, only='v')
            E.v_inc(E.v.max(out=max8[:], in_=ps[:, qc * 8:(qc + 1) * 8])
                    if E.w == 'v' else None)
            vsw(E)
            E.v_inc(E.v.tensor_add(sccol[:, qc:qc + 1], max8[:, 0:1],
                                   max8[:, 1:2]) if E.w == 'v' else None)
        scv = E.k.v

        mm_done = {}
        ev_done = {}
        for lcg in range(8):
            b = lcg % 2
            E.wait('t', mm_done.get(lcg - 2), only='g')
            ld = E.dma('g', fc1_blk[b][:],
                       fc1_w[:, lcg * 256:(lcg + 1) * 256].rearrange(
                           "(a p) f -> p a f", p=128))
            inst = None
            if E.w == 't':
                E.wait('g', ld)
                E.wait('v', ev_done.get(lcg - 2))
                if lcg == 0:
                    E.wait('v', scv)
            pbase = (lcg % 2) * 4
            for cc in range(NCC):
                for ls in range(2):
                    for qh in range(2):
                        if E.w == 't':
                            inst = E.e.matmul(
                                bank(pbase + ls * 2 + qh),
                                fc1_blk[b][:, cc, ls * 128:(ls + 1) * 128],
                                x1T[:, cc, qh * 512:(qh + 1) * 512],
                                start=(cc == 0), stop=(cc == NCC - 1))
            tmm = E.t_inc(inst)
            mm_done[lcg] = tmm
            evl = None
            for ls in range(2):
                lc = lcg * 2 + ls
                for qh in range(2):
                    E.wait('t', tmm, only='v')
                    evl = E.v_inc(E.v.tensor_copy(
                        xlT[:, lc, qh * 512:(qh + 1) * 512],
                        bank(pbase + ls * 2 + qh)) if E.w == 'v' else None)
            ev_done[lcg] = evl
        fc1_mm_last = mm_done[7]
        fc1_ev_last = ev_done[7]

        mm_done = {}
        ev_done = {}
        for mcg in range(8):
            b = mcg % 2
            if E.w == 'g':
                E.wait('t', mm_done.get(mcg - 2))
                if mcg < 2:
                    E.wait('t', mm_done.get(6) or fc1_mm_last)
            ld = E.dma('g', exp_blk[b][:],
                       exp_w[:, mcg * 256:(mcg + 1) * 256].rearrange(
                           "(a p) f -> p a f", p=128))
            inst = None
            if E.w == 't':
                E.wait('g', ld)
                E.wait('v', ev_done.get(mcg - 2))
                if mcg == 0:
                    E.wait('v', fc1_ev_last)
            pbase = (mcg % 2) * 4
            for lc in range(NLC):
                for ms in range(2):
                    for qh in range(2):
                        if E.w == 't':
                            inst = E.e.matmul(
                                bank(pbase + ms * 2 + qh),
                                exp_blk[b][:, lc, ms * 128:(ms + 1) * 128],
                                xlT[:, lc, qh * 512:(qh + 1) * 512],
                                start=(lc == 0), stop=(lc == NLC - 1))
            tmm = E.t_inc(inst)
            mm_done[mcg] = tmm
            evl = None
            for ms in range(2):
                mc = mcg * 2 + ms
                for qh in range(2):
                    E.wait('t', tmm, only='v')
                    evl = E.v_inc(E.v.tensor_copy(
                        moeT[:, mc, qh * 512:(qh + 1) * 512],
                        bank(pbase + ms * 2 + qh)) if E.w == 'v' else None)
            ev_done[mcg] = evl

    def ph_d2(E):
        mm_done = {}
        ev_done = {}
        hst = {}
        for icg in range(32):
            b = icg % 2
            E.wait('t', mm_done.get(icg - 2), only='g')
            ld = E.dma('g', up_blk[b][:],
                       up_w[:, icg * 256:(icg + 1) * 256].rearrange(
                           "(a p) f -> p a f", p=128))
            inst = None
            if E.w == 't':
                E.wait('g', ld)
                E.wait('v', ev_done.get(icg - 2))
            pbase = (icg % 2) * 4
            for cc in range(NCC):
                for isx in range(2):
                    for qh in range(2):
                        if E.w == 't':
                            inst = E.e.matmul(
                                bank(pbase + isx * 2 + qh),
                                up_blk[b][:, cc, isx * 128:(isx + 1) * 128],
                                x1T[:, cc, qh * 512:(qh + 1) * 512],
                                start=(cc == 0), stop=(cc == NCC - 1))
            tmm = E.t_inc(inst)
            mm_done[icg] = tmm
            evl = None
            for isx in range(2):
                ic = icg * 2 + isx
                for qh in range(2):
                    bb = isx * 2 + qh
                    if E.w == 'v':
                        E.wait('t', tmm)
                        E.wait('q', hst.get((icg - 1, isx, qh)))
                    E.v_inc(E.v.tensor_scalar_max(
                        htmp[bb][:], bank(pbase + isx * 2 + qh), 0.0)
                        if E.w == 'v' else None)
                    vsw(E)
                    evl = E.v_inc(E.v.tensor_mul(hev[bb][:], htmp[bb][:],
                                                 htmp[bb][:])
                                  if E.w == 'v' else None)
                    E.wait('v', evl, only='q')
                    hst[(icg, isx, qh)] = E.dma(
                        'q', s_hT[ic][:, qh * 512:(qh + 1) * 512], hev[bb][:])
            ev_done[icg] = evl

    def ph_d3(E):
        hld = {}
        for ic in range(NIC):
            dst = hT_a[:, ic, :] if ic < 48 else hT_b[:, ic - 48, :]
            hld[ic] = E.dma('q', dst, s_hT[ic])
        sub_mm = {}
        mmf = {}
        ev_done = {}
        sst = {}
        for foc in range(NCC):
            bfc = foc % 2
            E.wait('t', mmf.get(foc - 2), only='g')
            fld = E.dma('g', fc2_blk[bfc][:],
                        fc2_w[:, foc * 128:(foc + 1) * 128].rearrange(
                            "(a p) f -> p a f", p=128))
            inst = None
            pbase = (foc % 2) * 4
            if E.w == 't':
                E.wait('v', ev_done.get(foc - 2))
            subld = None
            for ic in range(NIC):
                sub = ic // 16
                sslot = foc * 4 + sub
                if ic % 16 == 0:
                    E.wait('t', sub_mm.get(sslot - 2), only='g')
                    subld = E.dma(
                        'g', dn_blk[sslot % 2][:],
                        down_w[sub * 2048:(sub + 1) * 2048,
                               foc * 128:(foc + 1) * 128].rearrange(
                            "(a p) f -> p a f", p=128))
                if E.w == 't':
                    if ic % 16 == 0:
                        E.wait('g', subld)
                    if foc == 0:
                        E.wait('q', hld[ic])
                ht = hT_a[:, ic, :] if ic < 48 else hT_b[:, ic - 48, :]
                for qh in range(2):
                    if E.w == 't':
                        inst = E.e.matmul(bank(pbase + qh),
                                          dn_blk[sslot % 2][:, ic % 16, :],
                                          ht[:, qh * 512:(qh + 1) * 512],
                                          start=(ic == 0),
                                          stop=(ic == NIC - 1))
                if ic % 16 == 15:
                    sub_mm[sslot] = E.t_inc(inst)
            for lc in range(NLC):
                if E.w == 't' and lc == 0:
                    E.wait('g', fld)
                for qh in range(2):
                    if E.w == 't':
                        inst = E.e.matmul(bank(pbase + 2 + qh),
                                          fc2_blk[bfc][:, lc, :],
                                          moeT[:, lc, qh * 512:(qh + 1) * 512],
                                          start=(lc == 0), stop=(lc == NLC - 1))
            tmm = E.t_inc(inst)
            mmf[foc] = tmm
            evl = None
            for kind in range(2):
                dst = (s_sh, s_rt)[kind]
                for qh in range(2):
                    bb = kind * 2 + qh
                    if E.w == 'v':
                        E.wait('t', tmm)
                        E.wait('q', sst.get((foc - 1, kind, qh)))
                    evl = E.v_inc(E.v.tensor_copy(
                        srev[bb][:], bank(pbase + kind * 2 + qh))
                        if E.w == 'v' else None)
                    E.wait('v', evl, only='q')
                    st = None
                    for j in range(4):
                        st = E.dma('q', dst[qh * 4 + j][foc],
                                   srev[bb][:, j * 128:(j + 1) * 128])
                    sst[(foc, kind, qh)] = st
            ev_done[foc] = evl

    def ph_d4(E):
        gld = E.dma('q', gam_t[:], ln_g)
        bld = E.dma('q', bet_t[:], ln_b)
        done = {}
        ost = {}
        trdone_at = {}
        ssq_at = {}
        for qc in range(NQC):
            b = qc % 2
            sh_nat = sh_nat2[b]
            rt_nat = rt_nat2[b]
            E.wait('t', trdone_at.get(qc - 1), only='q')
            ldh = E.dma('q', srcm_sh[0][:],
                        s_sh[qc].rearrange("a p q -> p a q"))
            ldr = E.dma('q', srcm_rt[0][:],
                        s_rt[qc].rearrange("a p q -> p a q"))
            E.wait('v', done.get(qc - 2), only='q')
            ldx = E.dma('q', x1d[b][:], s_x1[qc * 128:(qc + 1) * 128, :])
            if E.w == 'v':
                E.wait('q', max(ldx, bld))
            bx1 = E.v_inc(E.v.tensor_add(x1d[b][:], x1d[b][:], bet_t[:])
                          if E.w == 'v' else None)
            for grp in range(16):
                pb = grp % 2
                srcm = srcm_sh[0] if grp < 8 else srcm_rt[0]
                nat = sh_nat if grp < 8 else rt_nat
                g8 = grp % 8
                inst = None
                if E.w == 't':
                    E.wait('q', ldh if grp < 8 else ldr)
                    prevd = getattr(E, f'_d4tr{pb}', None)
                    if isinstance(prevd, tuple):
                        E.wait('s', prevd[1])
                    else:
                        E.wait('v', prevd)
                for s4 in range(4):
                    foc = g8 * 4 + s4
                    if E.w == 't':
                        inst = E.e.transpose(bank(pb)[:, s4 * 128:(s4 + 1) * 128],
                                             srcm[:, foc, :], ident_f[:])
                tmm = E.t_inc(inst)
                if grp % 2 == 0:
                    if E.w == 'v':
                        E.wait('t', tmm)
                        if grp == 0:
                            E.wait('s', ssq_at.get(qc - 2))
                    ev = E.v_inc(E.v.tensor_copy(
                        nat[:, g8 * 512:(g8 + 1) * 512], bank(pb))
                        if E.w == 'v' else None)
                else:
                    if E.w == 's':
                        E.wait('t', tmm)
                        if grp == 1:
                            E.wait('s', ssq_at.get(qc - 2))
                    ev = E.s_inc(E.s.copy(
                        nat[:, g8 * 512:(g8 + 1) * 512], bank(pb))
                        if E.w == 's' else None)
                    ev = ('s', ev)
                setattr(E, f'_d4tr{pb}', ev)
            trdone_at[qc] = E.k.t
            E.wait('s', E.k.s, only='v')
            vsw(E)
            cmb = E.v_inc(E.v.scalar_tensor_tensor(
                out=rt_nat[:], in0=rt_nat[:], scalar=sccol[:, qc:qc + 1],
                in1=sh_nat[:], op0=ALU.mult, op1=ALU.add,
                accum_out=stat[:, 0:1]) if E.w == 'v' else None)
            E.wait('v', cmb, only='s')
            vsw(E)
            ssq1, _ = ln_stats(E, rt_nat[:], 0, 0, d4trash, have_sum=True)
            E.wait('q', gld, only='v')
            E.v_inc(E.v.scalar_tensor_tensor(
                out=sh_nat[:], in0=rt_nat[:], scalar=sstat[:, 1:2],
                in1=gam_t[:], op0=ALU.subtract, op1=ALU.mult)
                if E.w == 'v' else None)
            vsw(E)
            p2 = E.v_inc(E.v.scalar_tensor_tensor(
                out=x1d[b][:], in0=sh_nat[:], scalar=stat[:, 5:6], in1=x1d[b][:],
                op0=ALU.mult, op1=ALU.add, accum_out=stat[:, 16:17])
                if E.w == 'v' else None)
            E.wait('v', p2, only='s')
            vsw(E)
            ssq2, _ = ln_stats(E, x1d[b][:], 16, 4, d4trash, have_sum=True)
            ssq_at[qc] = ssq2
            E.wait('q', ost.get(qc - 1), only='v')
            odone = E.v_inc(E.v.tensor_scalar(
                out=outt[:], in0=x1d[b][:], scalar1=sstat[:, 5:6],
                scalar2=stat[:, 21:22], op0=ALU.subtract, op1=ALU.mult)
                if E.w == 'v' else None)
            E.wait('v', odone, only='q')
            ost[qc] = E.dma('q', out_h[qc * 128:(qc + 1) * 128, :], outt[:])
            done[qc] = odone

    dump_specs = {
        'A0': [('xT_own', xT_own)],
        'A': [('KT', KT), ('Vt', Vt)],
        'B': [(f'att{h}', att[h]) for h in range(KVH)],
        'C2': [('x1T', x1T)],
        'D1': [('xlT', xlT), ('moeT', moeT), ('sccol', sccol)],
    }
    dump_drams = {}
    if dump and upto in dump_specs:
        for name, tile in dump_specs[upto]:
            dump_drams[name] = nc.dram_tensor(
                "dbg_" + name, list(tile.shape), tile.dtype,
                kind="ExternalOutput").ap()

    def ph_dump(E):
        for name, tile in dump_specs[upto]:
            E.dma('q', dump_drams[name], tile[:])

    phase_list = [('P0', ph_p0), ('A0', ph_a0), ('A3', ph_a3), ('A', ph_a),
                  ('B', ph_b), ('C1', ph_c1), ('C2', ph_c2), ('D1', ph_d1),
                  ('D2', ph_d2), ('D3', ph_d3), ('D4', ph_d4)]
    if upto is not None:
        names = [p[0] for p in phase_list]
        phase_list = phase_list[:names.index(upto) + 1]
        if dump and upto in dump_specs:
            phase_list.append(('DUMP', ph_dump))

    def run_all(which, eng):
        E = Em(which, eng, ctx)
        for name, fn in phase_list:
            fence(E)
            fn(E)
        if which == 'q':
            eng.wait_ge(ctx.sems['q'], 16 * E.k.q)
        fence(E)

    with estack, nc.Block() as block:
        @block.tensor
        def _(t):
            run_all('t', t)

        @block.vector
        def _(v):
            run_all('v', v)

        @block.scalar
        def _(s):
            run_all('s', s)

        @block.sync
        def _(q):
            run_all('q', q)

        @block.gpsimd
        def _(g):
            run_all('g', g)

    return nc


_IDENT_BF = np.eye(128).astype(ml_dtypes.bfloat16)
_IDENT_F = np.eye(128, dtype=np.float32)
_ONES_BF = np.ones((1, 512), dtype=ml_dtypes.bfloat16)


def make_in_maps(x, shared):
    in_maps = []
    for c in range(8):
        b = c // 4
        q0 = (c % 4) * QN
        m = dict(shared)
        m["xb"] = x[b]
        m["xq"] = np.ascontiguousarray(x[b][q0:q0 + QN])
        in_maps.append(m)
    return in_maps


def make_shared(L0_attn_w_q, L0_attn_b_q, L0_attn_w_k, L0_attn_b_k,
                L0_attn_w_v, L0_attn_b_v, L0_attn_w_o, L0_attn_b_o,
                L0_ffn_gate_w, L0_ffn_up_w, L0_ffn_down_w, L0_ffn_fc1_w,
                L0_ffn_fc2_w, L0_ffn_expert_w, L0_ffn_ln_gamma, L0_ffn_ln_beta):
    f = lambda a: np.ascontiguousarray(np.asarray(a, dtype=np.float32))
    return {
        "wq": f(L0_attn_w_q), "bq": f(L0_attn_b_q),
        "wk": f(L0_attn_w_k), "bk": f(L0_attn_b_k),
        "wv": f(L0_attn_w_v), "bv": f(L0_attn_b_v),
        "wo": f(L0_attn_w_o), "bo": f(L0_attn_b_o),
        "gate_w": f(L0_ffn_gate_w), "up_w": f(L0_ffn_up_w),
        "down_w": f(L0_ffn_down_w), "fc1_w": f(L0_ffn_fc1_w),
        "fc2_w": f(L0_ffn_fc2_w), "exp_w": f(L0_ffn_expert_w),
        "ln_g_rep": np.ascontiguousarray(np.broadcast_to(
            f(L0_ffn_ln_gamma)[None, :], (128, C))),
        "ln_b_rep": np.ascontiguousarray(np.broadcast_to(
            f(L0_ffn_ln_beta)[None, :], (128, C))),
        "ident_bf": _IDENT_BF, "ident_f32": _IDENT_F, "ones_bf": _ONES_BF,
    }


def kernel(L0_attn_x, L0_attn_mask, L0_attn_w_q, L0_attn_b_q, L0_attn_w_k,
           L0_attn_b_k, L0_attn_w_v, L0_attn_b_v, L0_attn_w_o, L0_attn_b_o,
           L0_ffn_gate_w, L0_ffn_up_w, L0_ffn_down_w, L0_ffn_fc1_w,
           L0_ffn_fc2_w, L0_ffn_expert_w, L0_ffn_ln_gamma, L0_ffn_ln_beta):
    x = np.ascontiguousarray(np.asarray(L0_attn_x, dtype=np.float32))
    B = x.shape[0]
    shared = make_shared(L0_attn_w_q, L0_attn_b_q, L0_attn_w_k, L0_attn_b_k,
                         L0_attn_w_v, L0_attn_b_v, L0_attn_w_o, L0_attn_b_o,
                         L0_ffn_gate_w, L0_ffn_up_w, L0_ffn_down_w,
                         L0_ffn_fc1_w, L0_ffn_fc2_w, L0_ffn_expert_w,
                         L0_ffn_ln_gamma, L0_ffn_ln_beta)
    nc = build()
    res = run_bass_kernel_spmd(nc, make_in_maps(x, shared),
                               core_ids=list(range(8)))
    out = np.empty((B, T, C), dtype=np.float32)
    for c in range(8):
        b = c // 4
        q0 = (c % 4) * QN
        out[b, q0:q0 + QN] = res.results[c]["out_shard"]
    return out
